# version 07
# baseline (speedup 1.0000x reference)
import numpy as np
from contextlib import ExitStack

import concourse.bass as bass
import concourse.mybir as mybir
from concourse.bass_utils import run_bass_kernel_spmd

F32 = mybir.dt.float32
BF16 = mybir.dt.bfloat16
ALU = mybir.AluOpType
ACT = mybir.ActivationFunctionType
AX = mybir.AxisListType.X

TOK = 2048
D = 1024
NT = TOK // 128
NCH = D // 128
NH = 16
DH = 64
IFF = 2048
NE0 = 8
E1, S1, F1 = 16, 64, 2048
SCALE1 = 1.0 / np.sqrt(DH * 2.0)
TEMP1 = 1.5
LN_EPS = 1e-5
RMS_EPS = 1e-6
FFN1_EPS = 1e-5
QK_BIAS = float(np.log(SCALE1 / TEMP1))


def build_core_program(debug=()):
    nc = bass.Bass("TRN2", target_bir_lowering=False)
    dbg = set(debug)
    dbg_out = {}

    def din(name, shape):
        return nc.dram_tensor(name, list(shape), F32, kind="ExternalInput").ap()

    x_in = din("x_in", [TOK, D])
    ident_in = din("ident_in", [128, 128])
    causal_in = din("causal_in", [128, 4 * 512])
    ones_in = din("ones_in", [1, 128])
    pat16_in = din("pat16_in", [16, NCH * 128])
    ld_cols_in = din("ld_cols", [128, NCH])
    qk_in = din("qk_in", [64, 32])
    nw_cols_in = din("nw_cols", [128, NCH])
    owT_in = din("owT", [D, D])
    wgsT_in = din("wgsT", [D, IFF])
    wusT_in = din("wusT", [D, IFF])
    wdsT_in = din("wdsT", [IFF, D])
    wgT_in = din("wgT", [NE0, D, IFF])
    wuT_in = din("wuT", [NE0, D, IFF])
    wdT_in = din("wdT", [NE0, IFF, D])
    wrT_in = din("wrT", [D, NE0])
    rb_in = din("rb_in", [1, NE0])
    wqT_in = din("wqT", [D, D])
    wkT_in = din("wkT", [D, D])
    wvT_in = din("wvT", [D, D])
    woT_in = din("woT", [D, D])
    bq_in = din("bq_in", [1, D])
    bk_in = din("bk_in", [1, D])
    bv_in = din("bv_in", [1, D])
    bo_in = din("bo_in", [1, D])
    se_in = din("se_in", [E1 * S1, D])
    ng_in = din("ng_in", [1, D])
    nb_in = din("nb_in", [1, D])
    sg_in = din("sg_in", [1, D])
    sb_in = din("sb_in", [1, D])
    sc1_in = din("sc1_in", [1, 1])
    wg1_in = din("wg1", [E1, D, F1])
    wu1_in = din("wu1", [E1, D, F1])
    wd1_in = din("wd1", [E1, F1, D])
    bg1_in = din("bg1", [1, E1 * F1])
    bu1_in = din("bu1", [1, E1 * F1])
    bd1_in = din("bd1", [1, E1 * D])

    x_out = nc.dram_tensor("x_out", [TOK, D], F32, kind="ExternalOutput").ap()
    def dbf(name, shape):
        return nc.dram_tensor(name, list(shape), BF16, kind="Internal").ap()
    wgsB = dbf("wgsB", [D, IFF])
    wusB = dbf("wusB", [D, IFF])
    wdsB = dbf("wdsB", [IFF, D])
    wgB = dbf("wgB", [NE0, D, IFF])
    wuB = dbf("wuB", [NE0, D, IFF])
    wdB = dbf("wdB", [NE0, IFF, D])
    owB = dbf("owB", [D, D])
    wqB = dbf("wqB", [D, D])
    wkB = dbf("wkB", [D, D])
    wvB = dbf("wvB", [D, D])
    woB = dbf("woB", [D, D])
    wg1B = dbf("wg1B", [E1, D, F1])
    wu1B = dbf("wu1B", [E1, D, F1])
    wd1B = dbf("wd1B", [E1, F1, D])
    g1_spill = nc.dram_tensor("g1_spill", [TOK, D], BF16, kind="Internal").ap()
    comb_sp = nc.dram_tensor("comb_sp", [128, NT * 1024], BF16, kind="Internal").ap()

    def dtap(name, shape, dtype=F32):
        if name in dbg:
            dbg_out[name] = nc.dram_tensor(
                "dbg_" + name, list(shape), dtype, kind="ExternalOutput").ap()
        return dbg_out.get(name)

    es = ExitStack()
    with es:
        _uniq = [0]

        def _un(name):
            _uniq[0] += 1
            return f"{name}_{_uniq[0]}"

        def sbuf(name, shape, dtype, ph=None):
            return (ph or es).enter_context(
                nc.sbuf_tensor(_un(name), shape, dtype))

        def psum(name, shape, dtype=F32, ph=None):
            return (ph or es).enter_context(
                nc.psum_tensor(_un(name), shape, dtype))

        class Tok:
            def __init__(self, name):
                self.h = es.enter_context(nc.semaphore(name))
                self.n = 0

            def inc(self, ins, k=1):
                ins.then_inc(self.h, k)
                self.n += k
                return self.n

        ident_b = sbuf("ident_b", [128, 128], BF16)
        ones_row = sbuf("ones_row", [1, 128], BF16)
        pat16_b = sbuf("pat16_b", [16, NCH * 128], BF16)
        ld_cols = sbuf("ld_cols_s", [128, NCH], F32)
        qk_b = sbuf("qk_b", [64, 32], BF16)
        nw_cols = sbuf("nw_cols_s", [128, NCH], F32)
        rb_row = sbuf("rb_row", [1, NE0], BF16)
        r_cols = sbuf("r_cols", [128, NT], F32)
        nbias_cols = sbuf("nbias_cols", [128, NT], F32)
        v_cols = sbuf("v_cols", [128, NT], F32)
        rho_cols = sbuf("rho_cols", [128, NT], F32)
        nb2_cols = sbuf("nb2_cols", [128, NT], F32)
        fmbuf = sbuf("fmbuf", [128, NCH * TOK], BF16)
        xacc = sbuf("xacc", [128, NT * D], F32)
        acc = xacc
        x_new = xacc

        constv = sbuf("constv", [128, 4], F32)

        T = Tok("sem_t")
        V = Tok("sem_v")
        S = Tok("sem_s")
        DO = Tok("sem_do")
        DC = Tok("sem_dc")
        DXS = [Tok(f"sem_dx{i}") for i in range(2)]
        DWS = [Tok(f"sem_dw{i}") for i in range(4)]

        block = es.enter_context(nc.Block())

        def p0_v(v):
            v.memset(constv[:, 0:1], LN_EPS)
            v.memset(constv[:, 1:2], RMS_EPS)
            v.memset(constv[:, 2:3], QK_BIAS)
            ins = v.drain()
            V.inc(ins)
        block.vector(p0_v)
        nc.const_aps.register(F32, LN_EPS, constv[:, 0:1])
        nc.const_aps.register(F32, RMS_EPS, constv[:, 1:2])
        nc.const_aps.register(F32, QK_BIAS, constv[:, 2:3])

        wr_b_g = sbuf("wr_b", [128, NCH * NE0], BF16)

        def p0_g(g):
            for dst, src in ((ident_b, ident_in),
                             (ones_row, ones_in), (pat16_b, pat16_in),
                             (qk_b, qk_in), (rb_row, rb_in),
                             (ld_cols, ld_cols_in), (nw_cols, nw_cols_in)):
                g.dma_start(out=dst[:], in_=src).then_inc(DC.h, 16)
                DC.n += 16
            g.dma_start(
                out=wr_b_g[:].rearrange("p (c e) -> p c e", c=NCH),
                in_=wrT_in.rearrange("(c p) e -> p c e", p=128)
            ).then_inc(DC.h, 16)
            DC.n += 16
        block.gpsimd(p0_g)
        P0_DC = DC.n

        PC = Tok("sem_pc")
        pc = {}

        def p0_pc(g):
            plan = [("wgs", wgsB, wgsT_in), ("wus", wusB, wusT_in),
                    ("wds", wdsB, wdsT_in)]
            for e in range(NE0):
                plan += [(f"wg{e}", wgB[e], wgT_in[e]),
                         (f"wu{e}", wuB[e], wuT_in[e]),
                         (f"wd{e}", wdB[e], wdT_in[e])]
            plan += [("ow", owB, owT_in), ("wq", wqB, wqT_in),
                     ("wk", wkB, wkT_in), ("wv", wvB, wvT_in)]
            for e in range(E1):
                plan += [(f"g1{e}", wg1B[e], wg1_in[e]),
                         (f"u1{e}", wu1B[e], wu1_in[e]),
                         (f"d1{e}", wd1B[e], wd1_in[e])]
            plan += [("wo", woB, woT_in)]
            for i, (key, dst, src) in enumerate(plan):
                if i:
                    g.wait_ge(PC.h, PC.n)
                g.dma_start(out=dst, in_=src).then_inc(PC.h, 16)
                PC.n += 16
                pc[key] = PC.n
        block.gpsimd(p0_pc)

        def hbm_loader(ph, hbm_ap, name, init_guard=()):
            xt = [sbuf(f"{name}{i}", [128, D], F32, ph) for i in range(2)]
            consumed = {}

            def load(t):
                q = t % 2

                def fsy(sy, t=t, q=q):
                    if t >= 2:
                        sem, val = consumed[t - 2]
                        sy.wait_ge(sem, val)
                    else:
                        for gw in init_guard:
                            sy.wait_ge(*gw)
                    sy.dma_start(out=xt[q][:],
                                 in_=hbm_ap[t * 128:(t + 1) * 128, :]
                                 ).then_inc(DXS[q].h, 16)
                    DXS[q].n += 16
                block.sync(fsy)
                return xt[q][:], (DXS[q].h, DXS[q].n)

            def mark(t, sem, val):
                consumed[t] = (sem, val)
            return load, mark

        def stats_pass(ph, src_of, mark, first_wait=()):
            c8 = sbuf("st_c8", [128, 8], F32, ph)
            x2 = sbuf("st_x2", [128, D], F32, ph)
            for t in range(NT):
                src, guard = src_of(t)

                def fv(v, t=t, src=src, guard=guard):
                    if guard is not None:
                        v.wait_ge(*guard)
                    if t == 0:
                        for fw in first_wait:
                            v.wait_ge(*fw)
                    v.tensor_mul(x2[:], src, src)
                    v.drain()
                    v.reduce_sum(c8[:, 0:1], src, axis=AX)
                    v.reduce_sum(c8[:, 1:2], x2[:], axis=AX)
                    v.drain()
                    v.tensor_scalar(c8[:, 2:3], c8[:, 0:1], 1.0 / D, None, ALU.mult)
                    v.tensor_scalar(c8[:, 4:5], c8[:, 1:2], 1.0 / D, None, ALU.mult)
                    v.drain()
                    v.tensor_mul(c8[:, 3:4], c8[:, 2:3], c8[:, 2:3])
                    v.drain()
                    ins = v.tensor_sub(v_cols[:, t:t + 1], c8[:, 4:5], c8[:, 3:4])
                    V.inc(ins)
                block.vector(fv)
                mark(t, V.h, V.n)
                vtok = V.n

                def fs(s, t=t, vtok=vtok):
                    s.wait_ge(V.h, vtok)
                    s.activation(c8[:, 5:6], v_cols[:, t:t + 1], ACT.Ln,
                                 bias=LN_EPS)
                    s.drain()
                    ins = s.activation(r_cols[:, t:t + 1], c8[:, 5:6],
                                       ACT.Exp, scale=-0.5)
                    S.inc(ins)
                block.scalar(fs)
                stok = S.n

                def fv2(v, t=t, stok=stok):
                    v.wait_ge(S.h, stok)
                    ins = v.scalar_tensor_tensor(
                        nbias_cols[:, t:t + 1], c8[:, 2:3], -1.0,
                        r_cols[:, t:t + 1], ALU.mult, ALU.mult)
                    V.inc(ins)
                block.vector(fv2)

        def fm_build(ph, src_of, mark, sc_cols, bs_cols, out_fm, gb=None,
                     first_wait=()):
            tm = [sbuf(f"fb_tm{i}", [128, D], BF16, ph) for i in range(2)]
            a32 = [sbuf(f"fb_a{i}", [128, D], F32, ph) for i in range(2)] \
                if gb else [None, None]
            ptr = [psum(f"fb_p{i}", [128, 1024], BF16, ph) for i in range(2)]
            ev = {}
            tmfree = {}
            for t in range(NT):
                q = t % 2
                src, guard = src_of(t)

                def fv0(v, t=t, q=q, src=src, guard=guard):
                    if guard is not None:
                        v.wait_ge(*guard)
                    if t == 0:
                        for fw in first_wait:
                            v.wait_ge(*fw)
                    if t >= 2:
                        v.wait_ge(T.h, tmfree[t - 2])
                    if gb:
                        v.tensor_scalar(a32[q][:], src,
                                        sc_cols[:, t:t + 1],
                                        bs_cols[:, t:t + 1],
                                        ALU.mult, ALU.add)
                        v.drain()
                        v.tensor_mul(a32[q][:], a32[q][:], gb[0][:])
                        v.drain()
                        ins = v.tensor_add(tm[q][:], a32[q][:], gb[1][:])
                    else:
                        ins = v.tensor_scalar(tm[q][:], src,
                                              sc_cols[:, t:t + 1],
                                              bs_cols[:, t:t + 1],
                                              ALU.mult, ALU.add)
                    V.inc(ins)
                block.vector(fv0)
                mark(t, V.h, V.n)
                wsem, wval = V.h, V.n

                def ft(tn, t=t, q=q, wsem=wsem, wval=wval):
                    tn.wait_ge(wsem, wval)
                    if t == 0:
                        for fw in first_wait:
                            tn.wait_ge(*fw)
                    if t >= 2:
                        tn.wait_ge(V.h, ev[t - 2])
                    for c in range(NCH):
                        ins = tn.transpose(ptr[q][:, c * 128:(c + 1) * 128],
                                           tm[q][:, c * 128:(c + 1) * 128],
                                           ident_b[:])
                    T.inc(ins)
                block.tensor(ft)
                tmfree[t] = T.n
                ttok = T.n

                def fv2(v, t=t, q=q, ttok=ttok):
                    v.wait_ge(T.h, ttok)
                    dst = out_fm[:].rearrange("p (c n) -> p c n", c=NCH)[
                        :, :, t * 128:(t + 1) * 128]
                    ins = v.tensor_copy(
                        dst, ptr[q][:].rearrange("p (c i) -> p c i", c=8))
                    V.inc(ins)
                block.vector(fv2)
                ev[t] = V.n
            return V.n

        def dump(name, buf, wait_sem, wait_val):
            if name not in dbg_out:
                return

            def fsy(sy, dprev=DO.n):
                sy.wait_ge(DO.h, dprev)
                sy.wait_ge(wait_sem, wait_val)
                sy.dma_start(out=dbg_out[name], in_=buf).then_inc(DO.h, 16)
                DO.n += 16
            block.sync(fsy)

            def fvbar(v, dn=DO.n):
                v.wait_ge(DO.h, dn)
                ins = v.drain()
                V.inc(ins)
            block.vector(fvbar)

            def fsbar(sc, dn=DO.n):
                sc.wait_ge(DO.h, dn)
                ins = sc.drain()
                S.inc(ins)
            block.scalar(fsbar)

        with ExitStack() as l0:
            cw_cols = sbuf("cw_cols", [128, NT * NE0], F32, l0)

            with ExitStack() as ph:
                load, mark = hbm_loader(ph, x_in, "p1a")
                stats_pass(ph, load, mark, first_wait=[(DC.h, P0_DC)])
            with ExitStack() as ph:
                load, mark = hbm_loader(ph, x_in, "p1b",
                                        init_guard=[(V.h, V.n)])
                p1_fm_tok = fm_build(ph, load, mark, r_cols, nbias_cols, fmbuf,
                                     first_wait=[(DC.h, P0_DC)])
            dtap("xn0_fm", [128, NCH * TOK], BF16)
            dump("xn0_fm", fmbuf[:], V.h, p1_fm_tok)
            dtap("r0", [128, NT])
            dump("r0", r_cols[:], V.h, V.n)

            with ExitStack() as ph:
                wr_b = wr_b_g
                wr_tok = P0_DC

                probs = [sbuf(f"p2_pr{i}", [128, NE0], F32, ph) for i in range(2)]
                mx8 = sbuf("p2_mx", [128, 8], F32, ph)
                zap = sbuf("p2_zap", [128, NE0], F32, ph)
                pr = [psum(f"p2_ps{i}", [128, NE0], F32, ph) for i in range(2)]
                evtok = {}
                sgtok = {}
                for t in range(NT):
                    q = t % 2

                    def ft(tn, t=t, q=q):
                        if t == 0:
                            tn.wait_ge(DC.h, wr_tok)
                            tn.wait_ge(V.h, p1_fm_tok)
                        if t >= 2:
                            tn.wait_ge(S.h, sgtok[t - 2])
                        tn.matmul(pr[q][:], ones_row[:], rb_row[:],
                                  start=True, stop=False)
                        ins = None
                        for c in range(NCH):
                            ins = tn.matmul(
                                pr[q][:],
                                fmbuf[:, c * TOK + t * 128:c * TOK + t * 128 + 128],
                                wr_b[:, c * NE0:(c + 1) * NE0],
                                start=False, stop=(c == NCH - 1))
                        T.inc(ins)
                    block.tensor(ft)
                    ttok = T.n

                    def fs(s, t=t, q=q, ttok=ttok):
                        s.wait_ge(T.h, ttok)
                        if t >= 2:
                            s.wait_ge(V.h, evtok[t - 2])
                        ins = s.activation(probs[q][:], pr[q][:], ACT.Sigmoid)
                        S.inc(ins)
                    block.scalar(fs)
                    sgtok[t] = S.n
                    stok = S.n

                    def fv(v, t=t, q=q, stok=stok):
                        v.wait_ge(S.h, stok)
                        v.max(out=mx8[:], in_=probs[q][:])
                        v.drain()
                        v.memset(mx8[:, 2:], 0.0)
                        v.drain()
                        v.match_replace(out=zap[:], in_to_replace=mx8[:],
                                        in_values=probs[q][:], imm_value=0.0)
                        v.drain()
                        ins = v.tensor_sub(cw_cols[:, t * NE0:(t + 1) * NE0],
                                           probs[q][:], zap[:])
                        V.inc(ins)
                    block.vector(fv)
                    evtok[t] = V.n
                p2_cw_tok = V.n
            dtap("cw", [128, NT * NE0])
            dump("cw", cw_cols[:], V.h, p2_cw_tok)

            with ExitStack() as ph:
                wg_s = sbuf("wg_s", [128, NCH * 1024], BF16, ph)
                wu_s = sbuf("wu_s", [128, NCH * 1024], BF16, ph)
                wd_s = sbuf("wd_s", [128, NCH * 1024], BF16, ph)
                h_b = sbuf("h_b", [128, NCH * TOK], BF16, ph)
                tsil = [sbuf(f"p3_sl{i}", [128, 512], BF16, ph) for i in range(2)]
                pg = [psum(f"p3_pg{i}", [128, 512], F32, ph) for i in range(2)]
                pu = [psum(f"p3_pu{i}", [128, 512], F32, ph) for i in range(2)]
                pd = [psum(f"p3_pd{i}", [128, 1024], F32, ph) for i in range(2)]
                FG, FU, FD = Tok("sem_fg"), Tok("sem_fu"), Tok("sem_fd")

                steps = [(e, hf) for e in range(-1, NE0) for hf in range(2)]
                NS = len(steps)

                def emit_wdma(si):
                    if si >= NS:
                        return
                    e, hf = steps[si]
                    pck = pc["wds"] if e == -1 else pc[f"wd{e}"]

                    def fg_(sy, si=si, e=e, hf=hf, te=T.n, tv=V.n, pck=pck):
                        sy.wait_ge(PC.h, pck)
                        if si >= 1:
                            sy.wait_ge(FG.h, si)
                            sy.wait_ge(FU.h, si)
                            sy.wait_ge(FD.h, si)
                        else:
                            sy.wait_ge(T.h, te)
                            sy.wait_ge(V.h, tv)
                        if e == -1:
                            gsrc = wgsB[:, hf * 1024:(hf + 1) * 1024]
                            usrc = wusB[:, hf * 1024:(hf + 1) * 1024]
                            dsrc = wdsB[hf * 1024:(hf + 1) * 1024, :]
                        else:
                            gsrc = wgB[e, :, hf * 1024:(hf + 1) * 1024]
                            usrc = wuB[e, :, hf * 1024:(hf + 1) * 1024]
                            dsrc = wdB[e, hf * 1024:(hf + 1) * 1024, :]
                        for k, (dst, src) in enumerate(
                                ((wg_s, gsrc), (wu_s, usrc), (wd_s, dsrc))):
                            sy.dma_start(
                                out=dst[:].rearrange("p (c i) -> p c i", c=NCH),
                                in_=src.rearrange("(c p) i -> p c i", p=128)
                            ).then_inc(DWS[k].h, 16)
                            DWS[k].n += 16
                    block.sync(fg_)
                emit_wdma(0)

                dn_ev = {}
                prev_silu_end = None
                prev_h_end = None
                for si, (e, hf) in enumerate(steps):
                    t0 = T.n
                    s0 = S.n
                    v0 = V.n

                    def ftn(tn, si=si, t0=t0, s0=s0, v0=v0,
                            pse=prev_silu_end, phe=prev_h_end):
                        tn.wait_ge(DWS[0].h, (si + 1) * 16)
                        tn.wait_ge(DWS[1].h, (si + 1) * 16)
                        if si == 0:
                            tn.wait_ge(V.h, p2_cw_tok)
                        for idx in range(32):
                            ic, n = divmod(idx, 4)
                            p = idx % 2
                            if idx >= 2:
                                tn.wait_ge(S.h, s0 + idx - 1)
                                tn.wait_ge(V.h, v0 + idx - 1)
                            elif pse is not None:
                                tn.wait_ge(S.h, pse)
                                tn.wait_ge(V.h, phe)
                            ins = None
                            for c in range(NCH):
                                ins = tn.matmul(
                                    pg[p][:],
                                    wg_s[:, c * 1024 + ic * 128:
                                         c * 1024 + ic * 128 + 128],
                                    fmbuf[:, c * TOK + n * 512:
                                          c * TOK + n * 512 + 512],
                                    start=(c == 0), stop=(c == NCH - 1))
                            T.inc(ins)
                            for c in range(NCH):
                                ins = tn.matmul(
                                    pu[p][:],
                                    wu_s[:, c * 1024 + ic * 128:
                                         c * 1024 + ic * 128 + 128],
                                    fmbuf[:, c * TOK + n * 512:
                                          c * TOK + n * 512 + 512],
                                    start=(c == 0), stop=(c == NCH - 1))
                            T.inc(ins)
                        tn.wait_ge(T.h, t0 + 64)
                        tn.nop().then_inc(FG.h, 1)
                        tn.nop().then_inc(FU.h, 1)
                    block.tensor(ftn)
                    FG.n += 1
                    FU.n += 1

                    def fsn(s, t0=t0, s0=s0, v0=v0, phe=prev_h_end):
                        for idx in range(32):
                            p = idx % 2
                            s.wait_ge(T.h, t0 + 2 * idx + 1)
                            if idx >= 2:
                                s.wait_ge(V.h, v0 + idx - 1)
                            elif phe is not None:
                                s.wait_ge(V.h, phe)
                            ins = s.activation(tsil[p][:], pg[p][:], ACT.Silu)
                            S.inc(ins)
                    block.scalar(fsn)

                    def fvn(v, t0=t0, s0=s0):
                        for idx in range(32):
                            ic, n = divmod(idx, 4)
                            p = idx % 2
                            v.wait_ge(T.h, t0 + 2 * idx + 2)
                            v.wait_ge(S.h, s0 + idx + 1)
                            ins = v.tensor_mul(
                                h_b[:, ic * TOK + n * 512:
                                    ic * TOK + n * 512 + 512],
                                tsil[p][:], pu[p][:])
                            V.inc(ins)
                    block.vector(fvn)
                    prev_silu_end = s0 + 32
                    prev_h_end = v0 + 32
                    vh_all = V.n
                    t1 = T.n

                    for t in range(NT):
                        p = t % 2

                        def ftd(tn, si=si, t=t, p=p, vh=vh_all):
                            if t == 0:
                                tn.wait_ge(DWS[2].h, (si + 1) * 16)
                                tn.wait_ge(V.h, vh)
                            if t >= 2:
                                sem, val = dn_ev[(si, t - 2)]
                                tn.wait_ge(sem, val)
                            elif si >= 1:
                                sem, val = dn_ev[(si - 1, NT - 2 + t)]
                                tn.wait_ge(sem, val)
                            ins = None
                            for ic in range(NCH):
                                for nn in range(2):
                                    ins = tn.matmul(
                                        pd[p][:, nn * 512:(nn + 1) * 512],
                                        h_b[:, ic * TOK + t * 128:
                                            ic * TOK + t * 128 + 128],
                                        wd_s[:, ic * 1024 + nn * 512:
                                             ic * 1024 + nn * 512 + 512],
                                        start=(ic == 0), stop=(ic == NCH - 1))
                            T.inc(ins)
                        block.tensor(ftd)
                        dtok = T.n
                        prev = dn_ev.get((si - 1, t))
                        if e == -1 and hf == 0:
                            def fse(s, t=t, p=p, dtok=dtok):
                                s.wait_ge(T.h, dtok)
                                ins = s.copy(acc[:, t * D:(t + 1) * D], pd[p][:])
                                S.inc(ins)
                            block.scalar(fse)
                            dn_ev[(si, t)] = (S.h, S.n)
                        else:
                            def fve(v, t=t, p=p, dtok=dtok, e=e, prev=prev):
                                v.wait_ge(T.h, dtok)
                                if prev is not None and prev[0] is S.h:
                                    v.wait_ge(*prev)
                                if e == -1:
                                    ins = v.tensor_add(
                                        acc[:, t * D:(t + 1) * D], pd[p][:],
                                        acc[:, t * D:(t + 1) * D])
                                else:
                                    ins = v.scalar_tensor_tensor(
                                        acc[:, t * D:(t + 1) * D], pd[p][:],
                                        cw_cols[:, t * NE0 + e:t * NE0 + e + 1],
                                        acc[:, t * D:(t + 1) * D],
                                        ALU.mult, ALU.add)
                                V.inc(ins)
                            block.vector(fve)
                            dn_ev[(si, t)] = (V.h, V.n)

                    def ftfree2(tn, tv=T.n):
                        tn.wait_ge(T.h, tv)
                        tn.nop().then_inc(FD.h, 1)
                    block.tensor(ftfree2)
                    FD.n += 1

                    emit_wdma(si + 1)
                p3_acc_tok = dn_ev[(NS - 1, NT - 1)]
                p3_t_end = T.n
                p3_s_end = S.n
                p3_v_end = V.n
            dtap("acc0", [128, NT * D])
            dump("acc0", acc[:], *p3_acc_tok)

            with ExitStack() as ph:
                lam_cols = sbuf("lam_cols", [128, NCH], F32, ph)
                c_cols = sbuf("c_cols", [128, NCH], F32, ph)
                c2_b = sbuf("c2_b", [128, NCH], BF16, ph)
                cnw = sbuf("cnw", [128, NCH], F32, ph)
                qkm = sbuf("qkm", [64, 16], BF16, ph)
                ones_col64 = sbuf("ones_c64", [64, 1], BF16, ph)
                c16_b = sbuf("c16_b", [16, 1], BF16, ph)
                scan_sc = sbuf("scan_sc", [128, TOK], BF16, ph)
                y2_sc = sbuf("y2_sc", [128, TOK], BF16, ph)
                rrow = sbuf("rrow", [1, TOK], F32, ph)
                rrow_b = sbuf("rrow_b", [1, TOK], BF16, ph)
                rm_cols = sbuf("rm_cols", [128, NT], F32, ph)
                ones11 = sbuf("ones11", [1, 1], BF16, ph)
                ow_b = sbuf("ow_b", [128, NCH * 1024], BF16, ph)
                silu_sc = [sbuf(f"p4_si{i}", [128, D], F32, ph) for i in range(2)]
                s_sc = [sbuf(f"p4_ss{i}", [128, D], F32, ph) for i in range(2)]

                OWT = Tok("sem_owt")

                def fg(sy):
                    sy.wait_ge(T.h, p3_t_end)
                    sy.wait_ge(V.h, p3_v_end)
                    sy.wait_ge(S.h, p3_s_end)
                    sy.wait_ge(PC.h, pc["ow"])
                    sy.dma_start(
                        out=ow_b[:].rearrange("p (c i) -> p c i", c=NCH),
                        in_=owB.rearrange("(c p) i -> p c i", p=128)
                    ).then_inc(OWT.h, 16)
                    OWT.n += 16
                block.sync(fg)

                def fsl(s):
                    s.wait_ge(DC.h, P0_DC)
                    ins = s.activation(lam_cols[:], ld_cols[:], ACT.Sigmoid)
                    S.inc(ins)
                block.scalar(fsl)
                lam_tok = S.n

                def fvq(v):
                    v.wait_ge(DC.h, P0_DC)
                    v.wait_ge(DXS[0].h, 32)
                    v.memset(ones_col64[:], 1.0)
                    v.memset(ones11[:], 1.0)
                    v.drain()
                    ins = v.tensor_mul(qkm[:], qk_b[:, 0:16], qk_b[:, 16:32])
                    V.inc(ins)
                block.vector(fvq)
                qkm_tok = V.n

                with ExitStack() as pp:
                    pc16 = psum("p4_pc16", [16, 1], F32, pp)
                    pcc = psum("p4_pcc", [128, NCH], F32, pp)
                    pss = psum("p4_pss", [1, TOK], F32, pp)
                    prc = psum("p4_prc", [128, NT], F32, pp)

                    def ftc(tn):
                        tn.wait_ge(V.h, qkm_tok)
                        tn.wait_ge(S.h, p3_s_end)
                        tn.wait_ge(V.h, p3_v_end)
                        ins = tn.matmul(pc16[:], qkm[:], ones_col64[:],
                                        start=True, stop=True)
                        T.inc(ins)
                    block.tensor(ftc)
                    c16_tok = T.n

                    def fvc(v):
                        v.wait_ge(T.h, c16_tok)
                        ins = v.tensor_copy(c16_b[:], pc16[:])
                        V.inc(ins)
                    block.vector(fvc)
                    c16b_tok = V.n

                    def ftc2(tn):
                        tn.wait_ge(V.h, c16b_tok)
                        ins = None
                        for c in range(NCH):
                            ins = tn.matmul(pcc[:, c:c + 1],
                                            pat16_b[:, c * 128:(c + 1) * 128],
                                            c16_b[:], start=True, stop=True)
                        T.inc(ins)
                    block.tensor(ftc2)
                    ccols_tok = T.n

                    def fvc2(v):
                        v.wait_ge(T.h, ccols_tok)
                        v.tensor_copy(c_cols[:], pcc[:])
                        v.drain()
                        v.tensor_mul(cnw[:], c_cols[:], nw_cols[:])
                        ins = v.tensor_mul(c2_b[:], c_cols[:], c_cols[:])
                        V.inc(ins)
                    block.vector(fvc2)
                    cprep_tok = V.n

                    ss_mm = {}
                    for c in range(NCH):
                        def fvs(v, c=c):
                            if c == 0:
                                v.wait_ge(S.h, lam_tok)
                                v.wait_ge(V.h, cprep_tok)
                                v.wait_ge(T.h, p3_t_end)
                            else:
                                v.wait_ge(T.h, ss_mm[c - 1])
                            v.tensor_tensor_scan(
                                scan_sc[:],
                                lam_cols[:, c:c + 1].to_broadcast([128, TOK]),
                                fmbuf[:, c * TOK:(c + 1) * TOK],
                                0.0, ALU.mult, ALU.add)
                            v.drain()
                            v.tensor_copy(fmbuf[:, c * TOK:(c + 1) * TOK],
                                          scan_sc[:])
                            v.drain()
                            ins = v.tensor_mul(y2_sc[:],
                                               fmbuf[:, c * TOK:(c + 1) * TOK],
                                               fmbuf[:, c * TOK:(c + 1) * TOK])
                            V.inc(ins)
                        block.vector(fvs)
                        sctok = V.n

                        def fts(tn, c=c, sctok=sctok):
                            tn.wait_ge(V.h, sctok)
                            ins = None
                            for n in range(4):
                                ins = tn.matmul(pss[:, n * 512:(n + 1) * 512],
                                                c2_b[:, c:c + 1],
                                                y2_sc[:, n * 512:(n + 1) * 512],
                                                start=(c == 0), stop=(c == NCH - 1))
                            T.inc(ins)
                        block.tensor(fts)
                        ss_mm[c] = T.n

                    def fsr(s):
                        s.wait_ge(T.h, ss_mm[NCH - 1])
                        s.activation(rrow[:], pss[:], ACT.Ln, scale=1.0 / D,
                                     bias=RMS_EPS)
                        s.drain()
                        ins = s.activation(rrow[:], rrow[:], ACT.Exp, scale=-0.5)
                        S.inc(ins)
                    block.scalar(fsr)
                    rrow_tok = S.n

                    def fvr(v):
                        v.wait_ge(S.h, rrow_tok)
                        ins = v.tensor_copy(rrow_b[:], rrow[:])
                        V.inc(ins)
                    block.vector(fvr)
                    rrowb_tok = V.n

                    def ftr(tn):
                        tn.wait_ge(V.h, rrowb_tok)
                        ins = None
                        for t in range(NT):
                            ins = tn.matmul(prc[:, t:t + 1],
                                            rrow_b[:, t * 128:(t + 1) * 128],
                                            ones11[:], start=True, stop=True)
                        T.inc(ins)
                    block.tensor(ftr)
                    prc_tok = T.n

                    def fvr2(v):
                        v.wait_ge(T.h, prc_tok)
                        ins = v.tensor_copy(rm_cols[:], prc[:])
                        V.inc(ins)
                    block.vector(fvr2)
                    rm_tok = V.n

                def fvo(v):
                    v.wait_ge(OWT.h, 16)
                    v.drain()
                    ins = None
                    for c in range(NCH):
                        ins = v.tensor_scalar(ow_b[:, c * 1024:(c + 1) * 1024],
                                              ow_b[:, c * 1024:(c + 1) * 1024],
                                              cnw[:, c:c + 1], None, ALU.mult)
                    V.inc(ins)
                block.vector(fvo)
                owb_tok = V.n

                ps = [psum(f"p4_ps{i}", [128, 1024], F32, ph) for i in range(2)]
                load, mark = hbm_loader(ph, x_in, "p4x",
                                        init_guard=[(V.h, V.n), (S.h, S.n),
                                                    (T.h, T.n)])
                fuse_s = {}
                fuse_vv = {}
                for t in range(NT):
                    q = t % 2
                    src, guard = load(t)

                    def ftm(tn, t=t, q=q):
                        if t == 0:
                            tn.wait_ge(V.h, owb_tok)
                        if t >= 2:
                            tn.wait_ge(S.h, fuse_s[t - 2])
                        ins = None
                        for c in range(NCH):
                            for nn in range(2):
                                ins = tn.matmul(
                                    ps[q][:, nn * 512:(nn + 1) * 512],
                                    fmbuf[:, c * TOK + t * 128:
                                          c * TOK + t * 128 + 128],
                                    ow_b[:, c * 1024 + nn * 512:
                                         c * 1024 + nn * 512 + 512],
                                    start=(c == 0), stop=(c == NCH - 1))
                        T.inc(ins)
                    block.tensor(ftm)
                    ttok = T.n

                    def fsf(s, t=t, q=q, ttok=ttok):
                        s.wait_ge(T.h, ttok)
                        if t == 0:
                            s.wait_ge(V.h, rm_tok)
                            s.wait_ge(*p3_acc_tok)
                        if t >= 2:
                            s.wait_ge(V.h, fuse_vv[t - 2])
                        s.activation(silu_sc[q][:], acc[:, t * D:(t + 1) * D],
                                     ACT.Silu)
                        ins = s.activation(s_sc[q][:], ps[q][:], ACT.Copy,
                                           scale=rm_cols[:, t:t + 1])
                        S.inc(ins)
                    block.scalar(fsf)
                    fuse_s[t] = S.n
                    stok = S.n

                    def fvf(v, t=t, q=q, stok=stok, src=src, guard=guard):
                        v.wait_ge(S.h, stok)
                        v.wait_ge(*guard)
                        v.tensor_mul(s_sc[q][:], s_sc[q][:], silu_sc[q][:])
                        v.drain()
                        ins = v.tensor_add(x_new[:, t * D:(t + 1) * D],
                                           src, s_sc[q][:])
                        V.inc(ins)
                    block.vector(fvf)
                    fuse_vv[t] = V.n
                    mark(t, V.h, V.n)
                p4_xnew_tok = V.n
            dtap("x_new", [128, NT * D])
            dump("x_new", x_new[:], V.h, p4_xnew_tok)

        with ExitStack() as l1:
            with ExitStack() as ph:
                def src_of(t):
                    return x_new[:, t * D:(t + 1) * D], None
                stats_pass(ph, src_of, lambda *a: None,
                           first_wait=[(S.h, S.n), (T.h, T.n)])
                rr16 = sbuf("p5_rr", [128, NT], F32, ph)

                def fv(v):
                    v.tensor_mul(rr16[:], r_cols[:], r_cols[:])
                    v.drain()
                    ins = v.tensor_mul(rr16[:], rr16[:], v_cols[:])
                    V.inc(ins)
                block.vector(fv)
                vtok = V.n

                def fs(s, vtok=vtok):
                    s.wait_ge(V.h, vtok)
                    s.activation(rr16[:], rr16[:], ACT.Ln, bias=FFN1_EPS)
                    s.drain()
                    ins = s.activation(rr16[:], rr16[:], ACT.Exp, scale=-0.5)
                    S.inc(ins)
                block.scalar(fs)
                stok = S.n

                def fv2(v, stok=stok):
                    v.wait_ge(S.h, stok)
                    v.tensor_mul(rho_cols[:], rr16[:], r_cols[:])
                    ins = v.tensor_mul(nb2_cols[:], rr16[:], nbias_cols[:])
                    V.inc(ins)
                block.vector(fv2)
                p5_tok = V.n

            with ExitStack() as p6:
                ngb = sbuf("ngb", [128, D], F32, p6)
                nbb = sbuf("nbb", [128, D], F32, p6)
                sgb = sbuf("sgb", [128, D], F32, p6)
                sbb = sbuf("sbb", [128, D], F32, p6)
                slots_fm = sbuf("slots_fm", [128, NCH * 1024], BF16, p6)
                subX = p6.enter_context(ExitStack())

                with ExitStack() as ph:
                    rows32 = sbuf("p6_r32", [1, 4 * D], F32, ph)
                    rows_b = sbuf("p6_rb", [1, 4 * D], BF16, ph)
                    sc32 = sbuf("p6_sc32", [1, 1], F32, ph)
                    sccol = sbuf("p6_sccol", [128, 1], F32, ph)
                    sc_b = sbuf("p6_scb", [1, 1], BF16, ph)
                    pbc = psum("p6_pbc", [128, 512], F32, ph)
                    pscc = psum("p6_pscc", [128, 1], F32, ph)

                    def fsy(sy, tv=V.n, ts=S.n, te=T.n):
                        sy.wait_ge(V.h, tv)
                        sy.wait_ge(S.h, ts)
                        sy.wait_ge(T.h, te)
                        for i, src in enumerate((ng_in, nb_in, sg_in, sb_in)):
                            sy.dma_start(out=rows32[:, i * D:(i + 1) * D],
                                         in_=src).then_inc(DXS[0].h, 16)
                            DXS[0].n += 16
                        sy.dma_start(out=sc32[:], in_=sc1_in).then_inc(DXS[0].h, 16)
                        DXS[0].n += 16
                    block.sync(fsy)
                    rows_dma = DXS[0].n

                    def fvb(v):
                        v.wait_ge(DXS[0].h, rows_dma)
                        v.tensor_copy(rows_b[:], rows32[:])
                        ins = v.tensor_copy(sc_b[:], sc32[:])
                        V.inc(ins)
                    block.vector(fvb)
                    rows_b_tok = V.n

                    def ftb(tn):
                        tn.wait_ge(V.h, rows_b_tok)
                        ins = tn.matmul(pscc[:], ones_row[:], sc_b[:],
                                        start=True, stop=True)
                        T.inc(ins)
                    block.tensor(ftb)
                    scc_tok = T.n

                    def fvb2(v):
                        v.wait_ge(T.h, scc_tok)
                        ins = v.tensor_copy(sccol[:], pscc[:])
                        V.inc(ins)
                    block.vector(fvb2)
                    sccol_tok = V.n

                    prev_ev = sccol_tok
                    for i, dstb in enumerate((ngb, nbb, sgb, sbb)):
                        for nn in range(2):
                            def ftb2(tn, i=i, nn=nn, prev_ev=prev_ev):
                                tn.wait_ge(V.h, prev_ev)
                                ins = tn.matmul(
                                    pbc[:],
                                    ones_row[:],
                                    rows_b[:, i * D + nn * 512:
                                           i * D + nn * 512 + 512],
                                    start=True, stop=True)
                                T.inc(ins)
                            block.tensor(ftb2)
                            ttok = T.n

                            def fvb3(v, nn=nn, dstb=dstb, ttok=ttok):
                                v.wait_ge(T.h, ttok)
                                ins = v.tensor_copy(
                                    dstb[:, nn * 512:(nn + 1) * 512], pbc[:])
                                V.inc(ins)
                            block.vector(fvb3)
                            prev_ev = V.n

                    def fvb4(v):
                        v.drain()
                        for nn in range(2):
                            v.tensor_scalar(sgb[:, nn * 512:(nn + 1) * 512],
                                            sgb[:, nn * 512:(nn + 1) * 512],
                                            sccol[:], None, ALU.mult)
                            ins = v.tensor_scalar(
                                sbb[:, nn * 512:(nn + 1) * 512],
                                sbb[:, nn * 512:(nn + 1) * 512],
                                sccol[:], None, ALU.mult)
                        V.inc(ins)
                    block.vector(fvb4)
                    bcast_tok = V.n

                with ExitStack() as ph:
                    def src_of(t):
                        return x_new[:, t * D:(t + 1) * D], None
                    xn2_tok = fm_build(ph, src_of, lambda *a: None,
                                       rho_cols, nb2_cols, fmbuf,
                                       gb=(ngb, nbb),
                                       first_wait=[(V.h, bcast_tok)])
                dtap("xn2_fm", [128, NCH * TOK], BF16)
                dump("xn2_fm", fmbuf[:], V.h, xn2_tok)

                se_fm = sbuf("se_fm", [128, NCH * 1024], BF16, subX)
                with ExitStack() as ph:
                    set_ = [sbuf(f"p6_se{i}", [128, D], F32, ph) for i in range(2)]
                    sec = sbuf("p6_sec", [128, 8], F32, ph)
                    sex2 = sbuf("p6_sex2", [128, D], F32, ph)
                    setm = [sbuf(f"p6_setm{i}", [128, D], BF16, ph)
                            for i in range(2)]
                    ptr6 = [psum(f"p6_ptr{i}", [128, 1024], BF16, ph)
                            for i in range(2)]
                    se_ev = {}
                    se_mark = {}
                    for ch in range(NCH):
                        q = ch % 2

                        def fsy(sy, ch=ch, q=q, tv=V.n, ts=S.n, te=T.n):
                            if ch >= 2:
                                sy.wait_ge(*se_mark[ch - 2])
                            else:
                                sy.wait_ge(V.h, tv)
                                sy.wait_ge(S.h, ts)
                                sy.wait_ge(T.h, te)
                            sy.dma_start(out=set_[q][:],
                                         in_=se_in[ch * 128:(ch + 1) * 128, :]
                                         ).then_inc(DXS[q].h, 16)
                            DXS[q].n += 16
                        block.sync(fsy)
                        dguard = (DXS[q].h, DXS[q].n)

                        def fv(v, ch=ch, q=q, dguard=dguard):
                            v.wait_ge(*dguard)
                            if ch == 0:
                                v.wait_ge(V.h, bcast_tok)
                            v.tensor_mul(sex2[:], set_[q][:], set_[q][:])
                            v.drain()
                            v.reduce_sum(sec[:, 0:1], set_[q][:], axis=AX)
                            v.reduce_sum(sec[:, 1:2], sex2[:], axis=AX)
                            v.drain()
                            v.tensor_scalar(sec[:, 2:3], sec[:, 0:1], 1.0 / D,
                                            None, ALU.mult)
                            v.tensor_scalar(sec[:, 4:5], sec[:, 1:2], 1.0 / D,
                                            None, ALU.mult)
                            v.drain()
                            v.tensor_mul(sec[:, 3:4], sec[:, 2:3], sec[:, 2:3])
                            v.drain()
                            ins = v.tensor_sub(sec[:, 4:5], sec[:, 4:5],
                                               sec[:, 3:4])
                            V.inc(ins)
                        block.vector(fv)
                        vtok = V.n

                        def fs(s, q=q, vtok=vtok):
                            s.wait_ge(V.h, vtok)
                            s.activation(sec[:, 5:6], sec[:, 4:5], ACT.Ln,
                                         bias=FFN1_EPS)
                            s.drain()
                            ins = s.activation(sec[:, 6:7], sec[:, 5:6],
                                               ACT.Exp, scale=-0.5)
                            S.inc(ins)
                        block.scalar(fs)
                        stok = S.n

                        def fv2(v, q=q, stok=stok):
                            v.wait_ge(S.h, stok)
                            ins = v.scalar_tensor_tensor(
                                sec[:, 7:8], sec[:, 2:3], -1.0,
                                sec[:, 6:7], ALU.mult, ALU.mult)
                            V.inc(ins)
                        block.vector(fv2)
                        vtok2 = V.n

                        def fv3(v, q=q, vtok2=vtok2):
                            v.wait_ge(V.h, vtok2)
                            v.drain()
                            v.tensor_scalar(set_[q][:], set_[q][:],
                                            sec[:, 6:7], sec[:, 7:8],
                                            ALU.mult, ALU.add)
                            v.drain()
                            v.tensor_mul(set_[q][:], set_[q][:], sgb[:])
                            v.drain()
                            ins = v.tensor_add(setm[q][:], set_[q][:], sbb[:])
                            V.inc(ins)
                        block.vector(fv3)
                        se_mark[ch] = (V.h, V.n)
                        vtok3 = V.n

                        def ft(tn, ch=ch, q=q, vtok3=vtok3):
                            tn.wait_ge(V.h, vtok3)
                            if ch >= 2:
                                tn.wait_ge(V.h, se_ev[ch - 2])
                            for c in range(NCH):
                                ins = tn.transpose(
                                    ptr6[q][:, c * 128:(c + 1) * 128],
                                    setm[q][:, c * 128:(c + 1) * 128],
                                    ident_b[:])
                            T.inc(ins)
                        block.tensor(ft)
                        ttok = T.n

                        def fv4(v, ch=ch, q=q, ttok=ttok):
                            v.wait_ge(T.h, ttok)
                            dst = se_fm[:].rearrange(
                                "p (c n) -> p c n", c=NCH)[
                                :, :, ch * 128:(ch + 1) * 128]
                            ins = v.tensor_copy(
                                dst, ptr6[q][:].rearrange("p (c i) -> p c i", c=8))
                            V.inc(ins)
                        block.vector(fv4)
                        se_ev[ch] = V.n
                    se_tok = V.n
                dtap("se_fm", [128, NCH * 1024], BF16)
                dump("se_fm", se_fm[:], V.h, se_tok)

                dispatch_tm = sbuf("dispatch_tm", [128, NT * 1024], BF16, subX)
                with ExitStack() as ph:
                    lg = sbuf("p6_lg", [128, TOK], F32, ph)
                    dsp_b = sbuf("p6_dspb", [128, TOK], BF16, ph)
                    nmcol = sbuf("p6_nm", [128, 1], F32, ph)
                    smcol = sbuf("p6_sm", [128, 1], F32, ph)
                    rscol = sbuf("p6_rs", [128, 1], F32, ph)
                    pl = [psum(f"p6_pl{i}", [128, 512], F32, ph) for i in range(2)]
                    ptr6b = [psum(f"p6_ptb{i}", [128, 1024], BF16, ph)
                             for i in range(2)]

                    prev_tr = None
                    prev_exp = None
                    for ch in range(NCH):
                        mmtok = {}
                        for n in range(4):
                            def ft(tn, ch=ch, n=n):
                                if ch == 0 and n == 0:
                                    tn.wait_ge(V.h, xn2_tok)
                                    tn.wait_ge(V.h, se_tok)
                                if n >= 2:
                                    tn.wait_ge(V.h, mmtok[n - 2][1])
                                elif ch >= 1:
                                    tn.wait_ge(V.h, mmtok_prev[n + 2][1])
                                ins = None
                                for k in range(NCH):
                                    ins = tn.matmul(
                                        pl[n % 2][:],
                                        se_fm[:, k * 1024 + ch * 128:
                                              k * 1024 + ch * 128 + 128],
                                        fmbuf[:, k * TOK + n * 512:
                                              k * TOK + n * 512 + 512],
                                        start=(k == 0), stop=(k == NCH - 1))
                                T.inc(ins)
                            block.tensor(ft)
                            ttok = T.n

                            def fv(v, n=n, ttok=ttok, prev_exp=prev_exp):
                                v.wait_ge(T.h, ttok)
                                if n == 0 and prev_exp is not None:
                                    v.wait_ge(S.h, prev_exp)
                                ins = v.tensor_copy(
                                    lg[:, n * 512:(n + 1) * 512], pl[n % 2][:])
                                V.inc(ins)
                            block.vector(fv)
                            mmtok[n] = (T.n, V.n)
                        mmtok_prev = mmtok

                        def fv5(v):
                            v.drain()
                            v.reduce_max(nmcol[:], lg[:], axis=AX, negate=True)
                            ins = v.drain()
                            V.inc(ins)
                        block.vector(fv5)
                        nm_tok = V.n

                        def fse(s, nm_tok=nm_tok, prev_tr=prev_tr):
                            s.wait_ge(V.h, nm_tok)
                            if prev_tr is not None:
                                s.wait_ge(T.h, prev_tr)
                            ins = s.activation(dsp_b[:], lg[:], ACT.Exp,
                                               bias=nmcol[:],
                                               accum_out=smcol[:])
                            S.inc(ins)
                        block.scalar(fse)
                        exp_tok = S.n
                        prev_exp = exp_tok

                        def fv6(v, exp_tok=exp_tok):
                            v.wait_ge(S.h, exp_tok)
                            v.reciprocal(rscol[:], smcol[:])
                            v.drain()
                            ins = v.tensor_scalar(dsp_b[:], dsp_b[:], rscol[:],
                                                  None, ALU.mult)
                            V.inc(ins)
                        block.vector(fv6)
                        dsp_tok = V.n

                        ptok = {}
                        for grp in range(2):
                            def ft2(tn, ch=ch, grp=grp, dsp_tok=dsp_tok):
                                tn.wait_ge(V.h, dsp_tok)
                                if (ch, grp) != (0, 0):
                                    pass
                                for j in range(8):
                                    t = grp * 8 + j
                                    ins = tn.transpose(
                                        ptr6b[grp][:, j * 128:(j + 1) * 128],
                                        dsp_b[:, t * 128:(t + 1) * 128],
                                        ident_b[:])
                                T.inc(ins)
                            block.tensor(ft2)
                            ttok2 = T.n

                            def fv7(v, ch=ch, grp=grp, ttok2=ttok2):
                                v.wait_ge(T.h, ttok2)
                                dst = dispatch_tm[:].rearrange(
                                    "p (t n) -> p t n", n=1024)[
                                    :, grp * 8:(grp + 1) * 8,
                                    ch * 128:(ch + 1) * 128]
                                ins = v.tensor_copy(
                                    dst, ptr6b[grp][:].rearrange(
                                        "p (c i) -> p c i", c=8))
                                V.inc(ins)
                            block.vector(fv7)
                            ptok[grp] = V.n
                            prev_tr = ttok2
                    dispatch_tok = V.n

                    cstage = [sbuf(f"p6_cst{i}", [128, 1024], BF16, ph)
                              for i in range(2)]
                    cb_ev = {}
                    cb_dma = {}
                    for t in range(NT):
                        mmtok = {}
                        for nn in range(2):
                            def ft(tn, t=t, nn=nn):
                                if t >= 1:
                                    tn.wait_ge(V.h, cb_mm_prev[nn])
                                ins = None
                                for k in range(NCH):
                                    ins = tn.matmul(
                                        pl[nn][:],
                                        fmbuf[:, k * TOK + t * 128:
                                              k * TOK + t * 128 + 128],
                                        se_fm[:, k * 1024 + nn * 512:
                                              k * 1024 + nn * 512 + 512],
                                        start=(k == 0), stop=(k == NCH - 1))
                                T.inc(ins)
                            block.tensor(ft)
                            ttok = T.n

                            def fv(v, nn=nn, ttok=ttok, prev_exp=prev_exp):
                                v.wait_ge(T.h, ttok)
                                if nn == 0:
                                    v.wait_ge(S.h, prev_exp)
                                ins = v.tensor_copy(
                                    lg[:, nn * 512:(nn + 1) * 512], pl[nn][:])
                                V.inc(ins)
                            block.vector(fv)
                            mmtok[nn] = V.n
                        cb_mm_prev = mmtok

                        def fv5(v, t=t):
                            v.drain()
                            v.reduce_max(nmcol[:], lg[:, 0:1024], axis=AX,
                                         negate=True)
                            ins = v.drain()
                            V.inc(ins)
                        block.vector(fv5)
                        nm_tok = V.n

                        def fse(s, nm_tok=nm_tok, prev_tr=prev_tr):
                            s.wait_ge(V.h, nm_tok)
                            s.wait_ge(T.h, prev_tr)
                            ins = s.activation(dsp_b[:, 0:1024], lg[:, 0:1024],
                                               ACT.Exp, bias=nmcol[:],
                                               accum_out=smcol[:])
                            S.inc(ins)
                        block.scalar(fse)
                        exp_tok = S.n
                        prev_exp = exp_tok

                        def fv6(v, exp_tok=exp_tok):
                            v.wait_ge(S.h, exp_tok)
                            v.reciprocal(rscol[:], smcol[:])
                            v.drain()
                            ins = v.tensor_scalar(dsp_b[:, 0:1024],
                                                  dsp_b[:, 0:1024], rscol[:],
                                                  None, ALU.mult)
                            V.inc(ins)
                        block.vector(fv6)
                        cmb_tok = V.n

                        def ft2(tn, t=t, cmb_tok=cmb_tok):
                            tn.wait_ge(V.h, cmb_tok)
                            if t >= 1:
                                tn.wait_ge(V.h, cb_ev[t - 1])
                            for j in range(8):
                                ins = tn.transpose(
                                    ptr6b[0][:, j * 128:(j + 1) * 128],
                                    dsp_b[:, j * 128:(j + 1) * 128],
                                    ident_b[:])
                            T.inc(ins)
                        block.tensor(ft2)
                        ttok2 = T.n

                        def fv7(v, t=t, ttok2=ttok2):
                            v.wait_ge(T.h, ttok2)
                            if t >= 2:
                                v.wait_ge(*cb_dma[t - 2])
                            ins = v.tensor_copy(cstage[t % 2][:], ptr6b[0][:])
                            V.inc(ins)
                        block.vector(fv7)
                        cb_ev[t] = V.n
                        vtok7 = V.n
                        prev_tr = ttok2

                        def fsy7(sy, t=t, vtok7=vtok7):
                            sy.wait_ge(V.h, vtok7)
                            sy.dma_start(
                                out=comb_sp[:, t * 1024:(t + 1) * 1024],
                                in_=cstage[t % 2][:]).then_inc(
                                    DXS[t % 2].h, 16)
                            DXS[t % 2].n += 16
                        block.sync(fsy7)
                        cb_dma[t] = (DXS[t % 2].h, DXS[t % 2].n)
                    combine_tok = V.n
                    cb_dma_done = [cb_dma[NT - 2], cb_dma[NT - 1]]
                dtap("dispatch_tm", [128, NT * 1024], BF16)
                dump("dispatch_tm", dispatch_tm[:], V.h, dispatch_tok)
                if dtap("combine_sp", [128, NT * 1024], BF16) is not None:
                    def fdc(sy, dprev=DO.n):
                        sy.wait_ge(DO.h, dprev)
                        sy.wait_ge(*cb_dma_done[0])
                        sy.wait_ge(*cb_dma_done[1])
                        sy.dma_start(out=dbg_out["combine_sp"],
                                     in_=comb_sp).then_inc(DO.h, 16)
                        DO.n += 16
                    block.sync(fdc)

                with ExitStack() as ph:
                    x2a = [sbuf(f"p6_x2a{i}", [128, D], F32, ph) for i in range(2)]
                    x2tm = [sbuf(f"p6_x2tm{i}", [128, D], BF16, ph)
                            for i in range(2)]
                    psl = [psum(f"p6_psl{k}", [128, 512], F32, ph)
                           for k in range(NCH)]
                    afft = {}
                    for gi in range(2 * NT):
                        eh, t = divmod(gi, NT)
                        q = gi % 2

                        def fva(v, gi=gi, t=t, q=q):
                            if gi == 0:
                                v.wait_ge(V.h, p5_tok)
                            if gi >= 2:
                                v.wait_ge(T.h, afft[gi - 2])
                            v.tensor_scalar(x2a[q][:],
                                            x_new[:, t * D:(t + 1) * D],
                                            rho_cols[:, t:t + 1],
                                            nb2_cols[:, t:t + 1],
                                            ALU.mult, ALU.add)
                            v.drain()
                            v.tensor_mul(x2a[q][:], x2a[q][:], ngb[:])
                            v.drain()
                            ins = v.tensor_add(x2tm[q][:], x2a[q][:], nbb[:])
                            V.inc(ins)
                        block.vector(fva)
                        vtok = V.n

                        def fta(tn, gi=gi, t=t, q=q, eh=eh, vtok=vtok):
                            tn.wait_ge(V.h, vtok)
                            if gi == 0:
                                tn.wait_ge(V.h, combine_tok)
                            if eh == 1 and t == 0:
                                tn.wait_ge(V.h, slots_ev0)
                            ins = None
                            for k in range(NCH):
                                ins = tn.matmul(
                                    psl[k][:],
                                    x2tm[q][:, k * 128:(k + 1) * 128],
                                    dispatch_tm[:, t * 1024 + eh * 512:
                                                t * 1024 + eh * 512 + 512],
                                    start=(t == 0), stop=(t == NT - 1))
                            T.inc(ins)
                        block.tensor(fta)
                        afft[gi] = T.n

                        if t == NT - 1:
                            mme = T.n

                            def fve(v, eh=eh, mme=mme):
                                v.wait_ge(T.h, mme)
                                ins = None
                                for k in range(NCH):
                                    ins = v.tensor_copy(
                                        slots_fm[:, k * 1024 + eh * 512:
                                                 k * 1024 + eh * 512 + 512],
                                        psl[k][:])
                                V.inc(ins)
                            block.vector(fve)
                            if eh == 0:
                                slots_ev0 = V.n
                    slots_tok = V.n
                dtap("slots_fm", [128, NCH * 1024], BF16)
                dump("slots_fm", slots_fm[:], V.h, slots_tok)
                subX.close()
                eout_b = sbuf("eout_b", [128, NCH * D], BF16, p6)

                with ExitStack() as ph:
                    wch = [sbuf(f"p6_wch{i}", [128, 8192], BF16, ph)
                           for i in range(2)]
                    gate_sb = sbuf("p6_gate", [64, F1], BF16, ph)
                    h1 = sbuf("p6_h1", [64, F1], BF16, ph)
                    h1T = sbuf("p6_h1T", [128, 1024], BF16, ph)
                    pg1 = [psum(f"p6_pg1{i}", [64, 512], F32, ph)
                           for i in range(4)]
                    pe1 = [psum(f"p6_pe1{i}", [64, 512], F32, ph)
                           for i in range(2)]
                    pth = [psum(f"p6_pth{i}", [128, 256], BF16, ph)
                           for i in range(2)]
                    FW = Tok("sem_fw")
                    EW = [Tok("sem_ew0"), Tok("sem_ew1")]
                    BDG = Tok("sem_bdg")
                    BDU = Tok("sem_bdu")
                    BDD = Tok("sem_bdd")
                    bge = sbuf("p6_bge", [1, F1], BF16, ph)
                    bue = sbuf("p6_bue", [1, F1], BF16, ph)
                    bde = sbuf("p6_bde", [1, D], BF16, ph)
                    nload = 0

                    def emit_edma(e):
                        nonlocal nload
                        if e >= E1:
                            return
                        srcs = []
                        for g in range(2):
                            srcs.append(("g", wg1B[e, g * 512:(g + 1) * 512, :]))
                        for g in range(2):
                            srcs.append(("u", wu1B[e, g * 512:(g + 1) * 512, :]))
                        for g in range(4):
                            srcs.append(("d", wd1B[e, g * 512:(g + 1) * 512, :]))

                        def fgb(g_, e=e, te=T.n, tv=V.n, ts=S.n):
                            if e >= 1:
                                g_.wait_ge(FW.h, 8 * e)
                            else:
                                g_.wait_ge(T.h, te)
                                g_.wait_ge(V.h, tv)
                                g_.wait_ge(S.h, ts)
                                g_.wait_ge(*cb_dma_done[0])
                                g_.wait_ge(*cb_dma_done[1])
                            g_.dma_start(out=bge[:],
                                         in_=bg1_in[:, e * F1:(e + 1) * F1]
                                         ).then_inc(BDG.h, 16)
                            g_.dma_start(out=bue[:],
                                         in_=bu1_in[:, e * F1:(e + 1) * F1]
                                         ).then_inc(BDU.h, 16)
                            g_.dma_start(out=bde[:],
                                         in_=bd1_in[:, e * D:(e + 1) * D]
                                         ).then_inc(BDD.h, 16)
                            BDG.n += 16
                            BDU.n += 16
                            BDD.n += 16
                        block.gpsimd(fgb)

                        def fg2(sy, e=e, srcs=srcs, te=T.n, tv=V.n):
                            nonlocal nload
                            sy.wait_ge(PC.h, pc[f"d1{e}"])
                            if e >= 1:
                                sy.wait_ge(FW.h, 8 * e)
                            else:
                                sy.wait_ge(T.h, te)
                                sy.wait_ge(V.h, tv)
                                sy.wait_ge(*cb_dma_done[0])
                                sy.wait_ge(*cb_dma_done[1])
                            WSEM = EW
                            for kind, src in srcs:
                                slot = nload % 2
                                if nload >= 2:
                                    pl = nload - 2
                                    r = pl % 8
                                    base = 8 * (pl // 8)
                                    req = base + (2 if r < 2 else
                                                  4 if r < 4 else
                                                  6 if r < 6 else 8)
                                    sy.wait_ge(FW.h, req)
                                w = src.shape[1]
                                dst = wch[slot][:, 0:4 * w].rearrange(
                                    "p (c i) -> p c i", c=4)
                                sy.dma_start(
                                    out=dst,
                                    in_=src.rearrange("(c p) i -> p c i", p=128)
                                ).then_inc(WSEM[slot].h, 16)
                                WSEM[slot].n += 16
                                nload += 1
                        block.sync(fg2)
                    emit_edma(0)

                    eout_ev = {}
                    nuse = 0
                    prev_h = None
                    for e in range(E1):
                        ce, re = e // 2, 64 * (e % 2)
                        def ftg(tn, e=e, nuse=nuse, prev_h=prev_h,
                                t_pre=T.n):
                            if e == 0:
                                tn.wait_ge(V.h, slots_tok)
                            else:
                                tn.wait_ge(V.h, prev_h)
                            tn.wait_ge(BDG.h, (e + 1) * 16)
                            for fs_ in range(4):
                                tn.matmul(pg1[fs_][:], ones_row[0:1, 0:64],
                                          bge[0:1, fs_ * 512:
                                              fs_ * 512 + 512],
                                          start=True, stop=False)
                            for k in range(NCH):
                                li = nuse + k // 4
                                gslot = li % 2
                                tn.wait_ge(EW[gslot].h, (li // 2 + 1) * 16)
                                ins = None
                                for fs_ in range(4):
                                    ins = tn.matmul(
                                        pg1[fs_][:],
                                        slots_fm[:, k * 1024 + e * 64:
                                                 k * 1024 + e * 64 + 64],
                                        wch[gslot][:, (k % 4) * F1 + fs_ * 512:
                                                   (k % 4) * F1 + fs_ * 512 + 512],
                                        start=False, stop=(k == NCH - 1))
                                if k % 4 == 3:
                                    T.inc(ins)
                            tn.wait_ge(T.h, t_pre + 2)
                            tn.nop().then_inc(FW.h, 2)
                        block.tensor(ftg)
                        FW.n += 2
                        gmm = T.n

                        def fsg(s, gmm=gmm, prev_h=prev_h):
                            s.wait_ge(T.h, gmm)
                            if prev_h is not None:
                                s.wait_ge(V.h, prev_h)
                            ins = None
                            for fs_ in range(4):
                                ins = s.activation(
                                    gate_sb[:, fs_ * 512:(fs_ + 1) * 512],
                                    pg1[fs_][:], ACT.Gelu)
                            S.inc(ins)
                        block.scalar(fsg)
                        gelu_tok = S.n

                        def ftu(tn, e=e, nuse=nuse, gelu_tok=gelu_tok,
                                t_pre=T.n):
                            tn.wait_ge(S.h, gelu_tok)
                            tn.wait_ge(BDU.h, (e + 1) * 16)
                            for fs_ in range(4):
                                tn.matmul(pg1[fs_][:], ones_row[0:1, 0:64],
                                          bue[0:1, fs_ * 512:
                                              fs_ * 512 + 512],
                                          start=True, stop=False)
                            for k in range(NCH):
                                li = nuse + 2 + k // 4
                                uslot = li % 2
                                tn.wait_ge(EW[uslot].h, (li // 2 + 1) * 16)
                                ins = None
                                for fs_ in range(4):
                                    ins = tn.matmul(
                                        pg1[fs_][:],
                                        slots_fm[:, k * 1024 + e * 64:
                                                 k * 1024 + e * 64 + 64],
                                        wch[uslot][:, (k % 4) * F1 + fs_ * 512:
                                                   (k % 4) * F1 + fs_ * 512 + 512],
                                        start=False, stop=(k == NCH - 1))
                                if k % 4 == 3:
                                    T.inc(ins)
                            tn.wait_ge(T.h, t_pre + 2)
                            tn.nop().then_inc(FW.h, 2)
                        block.tensor(ftu)
                        FW.n += 2
                        umm = T.n

                        def fvh(v, umm=umm):
                            v.wait_ge(T.h, umm)
                            ins = None
                            for fs_ in range(4):
                                ins = v.tensor_mul(
                                    h1[:, fs_ * 512:(fs_ + 1) * 512],
                                    gate_sb[:, fs_ * 512:(fs_ + 1) * 512],
                                    pg1[fs_][:])
                            V.inc(ins)
                        block.vector(fvh)
                        h_tok = V.n

                        htok = {}
                        for grp in range(4):
                            def fth(tn, grp=grp, h_tok=h_tok):
                                tn.wait_ge(V.h, h_tok)
                                if grp >= 2:
                                    tn.wait_ge(V.h, htok[grp - 2])
                                for j in range(4):
                                    piece = grp * 4 + j
                                    ins = tn.transpose(
                                        pth[grp % 2][:, j * 64:(j + 1) * 64],
                                        h1[0:64, piece * 128:(piece + 1) * 128],
                                        ident_b[0:64, 0:64])
                                T.inc(ins)
                            block.tensor(fth)
                            tht = T.n

                            def fvt(v, grp=grp, tht=tht):
                                v.wait_ge(T.h, tht)
                                ins = v.tensor_copy(
                                    h1T[:, grp * 256:(grp + 1) * 256],
                                    pth[grp % 2][:])
                                V.inc(ins)
                            block.vector(fvt)
                            htok[grp] = V.n
                        h1T_tok = V.n

                        def ftd1(tn, e=e, nuse=nuse, h1T_tok=h1T_tok,
                                 t_pre=T.n):
                            tn.wait_ge(V.h, h1T_tok)
                            if e >= 1:
                                tn.wait_ge(V.h, eout_ev[e - 1])
                            tn.wait_ge(BDD.h, (e + 1) * 16)
                            for nn in range(2):
                                tn.matmul(pe1[nn][:], ones_row[0:1, 0:64],
                                          bde[0:1, nn * 512:
                                              nn * 512 + 512],
                                          start=True, stop=False)
                            for k16 in range(16):
                                if k16 == 8:
                                    tn.wait_ge(T.h, t_pre + 2)
                                    tn.nop().then_inc(FW.h, 2)
                                li = nuse + 4 + k16 // 4
                                dslot = li % 2
                                tn.wait_ge(EW[dslot].h, (li // 2 + 1) * 16)
                                ins = None
                                for nn in range(2):
                                    ins = tn.matmul(
                                        pe1[nn][:],
                                        h1T[:, k16 * 64:(k16 + 1) * 64],
                                        wch[dslot][:, (k16 % 4) * 1024 + nn * 512:
                                                   (k16 % 4) * 1024 + nn * 512 + 512],
                                        start=False, stop=(k16 == 15))
                                if k16 % 4 == 3:
                                    T.inc(ins)
                            tn.wait_ge(T.h, t_pre + 4)
                            tn.nop().then_inc(FW.h, 2)
                        block.tensor(ftd1)
                        FW.n += 4
                        dmm = T.n

                        def fvev(v, e=e, ce=ce, re=re, dmm=dmm):
                            v.wait_ge(T.h, dmm)
                            ins = None
                            for nn in range(2):
                                ins = v.tensor_copy(
                                    eout_b[re:re + 64,
                                           ce * 1024 + nn * 512:
                                           ce * 1024 + nn * 512 + 512],
                                    pe1[nn][:])
                            V.inc(ins)
                        block.vector(fvev)
                        eout_ev[e] = V.n
                        prev_h = h_tok
                        nuse += 8
                        emit_edma(e + 1)
                    eout_tok = V.n
                dtap("eout", [128, NCH * D], BF16)
                dump("eout", eout_b[:], V.h, eout_tok)

                with ExitStack() as ph:
                    CR = [Tok("sem_cr0"), Tok("sem_cr1")]
                    GS = [Tok("sem_gs0"), Tok("sem_gs1")]
                    g1t = [sbuf(f"p6_g1{i}", [128, D], BF16, ph) for i in range(2)]
                    cst2 = [sbuf(f"p6_cs2{i}", [128, 1024], BF16, ph)
                            for i in range(2)]
                    pcm = [psum(f"p6_pcm{i}", [128, 1024], F32, ph)
                           for i in range(2)]
                    g1_dma = {}
                    cmb_tok2 = {}
                    for t in range(NT):
                        q = t % 2

                        def fsyc(sy, t=t, q=q, te=T.n, tv=V.n, ts=S.n):
                            if t < 2:
                                sy.wait_ge(*cb_dma_done[0])
                                sy.wait_ge(*cb_dma_done[1])
                                sy.wait_ge(T.h, te)
                                sy.wait_ge(V.h, tv)
                                sy.wait_ge(S.h, ts)
                            else:
                                sem, val = cmb_tok2[t - 2]
                                sy.wait_ge(T.h, val)
                            sy.dma_start(out=cst2[q][:],
                                         in_=comb_sp[:, t * 1024:(t + 1) * 1024]
                                         ).then_inc(CR[q].h, 16)
                            CR[q].n += 16
                        block.sync(fsyc)
                        crel = (CR[q].h, CR[q].n)

                        def ftg1(tn, t=t, q=q, crel=crel):
                            tn.wait_ge(*crel)
                            if t == 0:
                                tn.wait_ge(V.h, eout_tok)
                            if t >= 2:
                                sem, val = g1_dma[t - 2]
                                tn.wait_ge(sem, val)
                            ins = None
                            for k in range(NCH):
                                for nn in range(2):
                                    ins = tn.matmul(
                                        pcm[q][:, nn * 512:(nn + 1) * 512],
                                        cst2[q][:, k * 128:(k + 1) * 128],
                                        eout_b[:, k * 1024 + nn * 512:
                                               k * 1024 + nn * 512 + 512],
                                        start=(k == 0), stop=(k == NCH - 1))
                            T.inc(ins)
                        block.tensor(ftg1)
                        ttok = T.n
                        cmb_tok2[t] = (T.h, ttok)

                        def fsg1(s, t=t, q=q, ttok=ttok):
                            s.wait_ge(T.h, ttok)
                            if t >= 2:
                                s.wait_ge(*g1_dma[t - 2])
                            ins = s.activation(g1t[q][:], pcm[q][:], ACT.Silu)
                            S.inc(ins)
                        block.scalar(fsg1)
                        stok = S.n

                        def fsyg(sy, t=t, q=q, stok=stok):
                            sy.wait_ge(S.h, stok)
                            sy.dma_start(out=g1_spill[t * 128:(t + 1) * 128, :],
                                         in_=g1t[q][:]).then_inc(GS[q].h, 16)
                            GS[q].n += 16
                        block.sync(fsyg)
                        g1_dma[t] = (GS[q].h, GS[q].n)
                    g1_done = [g1_dma[NT - 2], g1_dma[NT - 1]]

            attnout = sbuf("attnout", [128, NCH * TOK], BF16, l1)
            causal_b = sbuf("causal_b", [128, 4 * 512], BF16, l1)

            CAU = Tok("sem_cau")

            def fgc(g, te=T.n, tv=V.n, ts=S.n):
                g.wait_ge(T.h, te)
                g.wait_ge(V.h, tv)
                g.wait_ge(S.h, ts)
                g.dma_start(out=causal_b[:], in_=causal_in).then_inc(CAU.h, 16)
                CAU.n += 16
            block.gpsimd(fgc)

            with ExitStack() as ph:
                def src_of(t):
                    return x_new[:, t * D:(t + 1) * D], None
                xn1_tok = fm_build(ph, src_of, lambda *a: None,
                                   r_cols, nbias_cols, fmbuf,
                                   first_wait=[g1_done[0], g1_done[1],
                                               (S.h, S.n)])
            dtap("xn1_fm", [128, NCH * TOK], BF16)
            dump("xn1_fm", fmbuf[:], V.h, xn1_tok)

            with ExitStack() as p7:
                Qf = sbuf("Qf", [128, 2 * TOK], BF16, p7)
                Kf = sbuf("Kf", [128, 2 * TOK], BF16, p7)
                Vp = sbuf("Vp", [128, NT * 260], BF16, p7)

                wsl_last = {}
                srd_last = None
                sd_last = None
                ao_last = None
                wsl = [sbuf(f"p7_w{i}", [128, NCH * 256], BF16, p7)
                       for i in range(2)]
                brow2 = [sbuf(f"p7_brb{i}", [1, 256], BF16, p7)
                         for i in range(2)]
                qsq = sbuf("p7_qsq", [128, 256], F32, p7)
                ssq4 = [sbuf(f"p7_ssq{i}", [128, 4], F32, p7)
                        for i in range(4)]
                rn84 = [sbuf(f"p7_rn8{i}", [128, 4], F32, p7)
                        for i in range(4)]
                qhat4 = [sbuf(f"p7_qhat{i}", [128, 256], BF16, p7)
                         for i in range(4)]
                exp_sb = sbuf("p7_exps", [128, 16 * 512], BF16, p7)
                dn4 = sbuf("p7_dn", [1, 2048], F32, p7)
                rd4 = sbuf("p7_rd", [1, 2048], BF16, p7)
                prd_sb4 = sbuf("p7_prds", [64, 2048], BF16, p7)
                for hq in range(4):
                    pq2 = p7.enter_context(ExitStack())
                    pqk = [psum(f"p7_pqk{i}", [128, 256], F32, pq2)
                           for i in range(4)]
                    ptrbuf = psum("p7_ptr", [128, 1024], BF16, pq2)
                    ptr7 = [ptrbuf[:, i * 256:(i + 1) * 256] for i in range(4)]

                    qkv_ev = {}
                    rn_done = {}
                    qhat_rd = {}
                    pq_free = {}
                    for which, (b_in, isv) in enumerate(
                            ((bq_in, False), (bk_in, False), (bv_in, True))):
                        wq_ = which % 2
                        wB = (wqB, wkB, wvB)[which]
                        WQL = Tok(f"sem_wql{hq}_{which}")
                        BRL = Tok(f"sem_brl{hq}_{which}")

                        def fgw(sy, hq=hq, wq_=wq_, wB=wB, WQL=WQL,
                                which=which, te=T.n, tv=V.n, ts=S.n,
                                wfree=wsl_last.get(which % 2)):
                            sy.wait_ge(PC.h, pc[("wq", "wk", "wv")[which]])
                            if wfree is not None:
                                sy.wait_ge(T.h, wfree)
                            else:
                                sy.wait_ge(T.h, te)
                                sy.wait_ge(V.h, tv)
                                sy.wait_ge(S.h, ts)
                            sy.dma_start(
                                out=wsl[wq_][:].rearrange("p (c i) -> p c i",
                                                          c=NCH),
                                in_=wB[:, hq * 256:(hq + 1) * 256]
                                .rearrange("(c p) i -> p c i", p=128)
                            ).then_inc(WQL.h, 16)
                            WQL.n += 16
                        block.sync(fgw)

                        def fgwb(g, hq=hq, b_in=b_in, BRL=BRL, wq_=wq_,
                                 te=T.n, wfree=wsl_last.get(which % 2)):
                            if wfree is not None:
                                g.wait_ge(T.h, wfree)
                            else:
                                g.wait_ge(T.h, te)
                            g.dma_start(out=brow2[wq_][:],
                                        in_=b_in[:, hq * 256:(hq + 1) * 256]
                                        ).then_inc(BRL.h, 16)
                            BRL.n += 16
                        block.gpsimd(fgwb)

                        for t in range(NT):
                            q = t % 4
                            wt = which * NT + t

                            def ftqkv(tn, t=t, q=q, wq_=wq_, which=which,
                                      hq=hq, WQL=WQL, BRL=BRL, wt=wt,
                                      ao_prev=ao_last, srd_prev=srd_last):
                                if t == 0:
                                    tn.wait_ge(WQL.h, 16)
                                    tn.wait_ge(BRL.h, 16)
                                    if hq == 0 and which == 0:
                                        tn.wait_ge(V.h, xn1_tok)
                                    if which == 0 and ao_prev is not None:
                                        tn.wait_ge(V.h, ao_prev)
                                        tn.wait_ge(S.h, srd_prev)
                                if wt >= 4 and (wt - 4) in pq_free:
                                    tn.wait_ge(V.h, pq_free[wt - 4])
                                tn.matmul(pqk[q][:], ones_row[:],
                                          brow2[wq_][:],
                                          start=True, stop=False)
                                ins = None
                                for k in range(NCH):
                                    ins = tn.matmul(
                                        pqk[q][:],
                                        fmbuf[:, k * TOK + t * 128:
                                              k * TOK + t * 128 + 128],
                                        wsl[wq_][:, k * 256:(k + 1) * 256],
                                        start=False, stop=(k == NCH - 1))
                                T.inc(ins)
                            block.tensor(ftqkv)
                            ttok = T.n

                            if not isv:
                                bias_const = QK_BIAS if which == 0 else 0.0

                                def fsA(sc, q=q, ttok=ttok,
                                        rn_prev=rn_done.get(wt - 4)):
                                    sc.wait_ge(T.h, ttok)
                                    if rn_prev is not None:
                                        sc.wait_ge(V.h, rn_prev)
                                    ins = None
                                    for hh in range(4):
                                        ins = sc.activation(
                                            qsq[:, hh * 64:(hh + 1) * 64],
                                            pqk[q][:, hh * 64:(hh + 1) * 64],
                                            ACT.Square,
                                            accum_out=ssq4[q][:, hh:hh + 1])
                                    S.inc(ins)
                                block.scalar(fsA)
                                sa_tok = S.n

                                def fvmx(v, q=q, sa_tok=sa_tok):
                                    v.wait_ge(S.h, sa_tok)
                                    ins = v.tensor_scalar_max(
                                        ssq4[q][:], ssq4[q][:], 1e-16)
                                    V.inc(ins)
                                block.vector(fvmx)
                                vt1 = V.n
                                rn_done[wt] = vt1

                                def fsq(s, q=q, vt1=vt1,
                                        bias_const=bias_const):
                                    s.wait_ge(V.h, vt1)
                                    s.activation(rn84[q][:], ssq4[q][:],
                                                 ACT.Ln)
                                    s.drain()
                                    ins = s.activation(rn84[q][:],
                                                       rn84[q][:],
                                                       ACT.Exp, scale=-0.5,
                                                       bias=bias_const)
                                    S.inc(ins)
                                block.scalar(fsq)
                                st1 = S.n

                                def fvq2(v, q=q, st1=st1,
                                         qh_prev=qhat_rd.get(wt - 4)):
                                    v.wait_ge(S.h, st1)
                                    if qh_prev is not None:
                                        v.wait_ge(T.h, qh_prev)
                                    ins = None
                                    for hh in range(4):
                                        ins = v.tensor_scalar(
                                            qhat4[q][:, hh * 64:
                                                      (hh + 1) * 64],
                                            pqk[q][:, hh * 64:(hh + 1) * 64],
                                            rn84[q][:, hh:hh + 1], None,
                                            ALU.mult)
                                    V.inc(ins)
                                block.vector(fvq2)
                                vt2 = V.n
                                pq_free[wt] = vt2

                                def ftt(tn, q=q, vt2=vt2, t=t):
                                    tn.wait_ge(V.h, vt2)
                                    if t >= 4:
                                        tn.wait_ge(V.h, qkv_ev[t - 4])
                                    for j in range(2):
                                        ins = tn.transpose(
                                            ptrbuf[:, q * 256 + j * 128:
                                                   q * 256 + j * 128 + 128],
                                            qhat4[q][:, j * 128:
                                                      (j + 1) * 128],
                                            ident_b[:])
                                    T.inc(ins)
                                block.tensor(ftt)
                                tt2 = T.n
                                qhat_rd[wt] = tt2
                                dstbuf = Qf if which == 0 else Kf

                                def fvq3(v, t=t, q=q, tt2=tt2, dstbuf=dstbuf):
                                    v.wait_ge(T.h, tt2)
                                    dst = dstbuf[:].rearrange(
                                        "p (c n) -> p c n", c=2)[
                                        :, :, t * 128:(t + 1) * 128]
                                    ins = v.tensor_copy(
                                        dst, ptr7[q].rearrange(
                                            "p (c i) -> p c i", c=2))
                                    V.inc(ins)
                                block.vector(fvq3)
                                qkv_ev[t] = V.n
                            else:
                                def fvv(v, t=t, q=q, ttok=ttok):
                                    v.wait_ge(T.h, ttok)
                                    dst = Vp[:, t * 260:(t + 1) * 260
                                             ].rearrange(
                                        "p (h x) -> p h x", h=4)[:, :, 0:64]
                                    v.tensor_copy(
                                        dst, pqk[q][:].rearrange(
                                            "p (c i) -> p c i", c=4))
                                    ones_dst = Vp[:, t * 260:(t + 1) * 260
                                                  ].rearrange(
                                        "p (h x) -> p h x", h=4)[:, :, 64:65]
                                    v.memset(ones_dst, 1.0)
                                    ins = v.drain()
                                    V.inc(ins)
                                block.vector(fvv)
                                qkv_ev[t] = V.n
                                pq_free[wt] = V.n
                        wsl_last[which % 2] = T.n
                    qkv_done = V.n
                    pq2.close()

                    with ExitStack() as pa:
                        psc = [psum(f"p7_psc{i}", [128, 512], F32, pa)
                               for i in range(2)]
                        pav = [psum(f"p7_pav{i}", [128, 512], F32, pa)
                               for i in range(4)]
                        prd = psum("p7_prd", [64, 512], F32, pa)
                        ao_ev = {}
                        gidx = 0
                        for h8 in range(4):
                            cc = 2 * hq + h8 // 2
                            row0 = 64 * (h8 % 2)
                            ccl = h8 // 2
                            av_tok = {}
                            for js in range(4):
                                nk = 4 * js + 4
                                npair = (nk + 1) // 2
                                t0 = T.n
                                s0 = S.n

                                def fA(tn, h8=h8, js=js, nk=nk, ccl=ccl,
                                       row0=row0, t0=t0, s0=s0, gidx=gidx,
                                       qkv_done=qkv_done, hq=hq):
                                    if gidx == 0:
                                        tn.wait_ge(V.h, qkv_done)
                                    for kt in range(nk):
                                        pp = kt % 2
                                        if kt >= 2:
                                            tn.wait_ge(S.h, s0 + kt - 1)
                                        elif gidx > 0 or hq > 0:
                                            tn.wait_ge(S.h, s0)
                                        ins = tn.matmul(
                                            psc[pp][:],
                                            Kf[row0:row0 + 64,
                                               ccl * TOK + kt * 128:
                                               ccl * TOK + kt * 128 + 128],
                                            Qf[row0:row0 + 64,
                                               ccl * TOK + js * 512:
                                               ccl * TOK + js * 512 + 512],
                                            start=True, stop=True)
                                        T.inc(ins)
                                block.tensor(fA)

                                def fS(sc, nk=nk, t0=t0):
                                    for kt in range(nk):
                                        sc.wait_ge(T.h, t0 + kt + 1)
                                        ins = sc.activation(
                                            exp_sb[:, kt * 512:
                                                   (kt + 1) * 512],
                                            psc[kt % 2][:], ACT.Exp)
                                        S.inc(ins)
                                block.scalar(fS)

                                mask_tok = {}

                                def fM(v, js=js, nk=nk, s0=s0, hq=hq, h8=h8):
                                    for kt in range(4 * js, nk):
                                        v.wait_ge(S.h, s0 + kt + 1)
                                        if hq == 0 and h8 == 0 and js == 0 \
                                                and kt == 0:
                                            v.wait_ge(CAU.h, 16)
                                        rr = kt * 128 - js * 512
                                        ins = v.tensor_mul(
                                            exp_sb[:, kt * 512:(kt + 1) * 512],
                                            exp_sb[:, kt * 512:(kt + 1) * 512],
                                            causal_b[:, (rr // 128) * 512:
                                                     (rr // 128) * 512 + 512])
                                        V.inc(ins)
                                        mask_tok[kt] = V.n
                                block.vector(fM)

                                def fB(tn, h8=h8, js=js, nk=nk, s0=s0,
                                       gidx=gidx, mask_tok=mask_tok,
                                       prev=ao_ev.get(gidx - 4)):
                                    for kt in range(nk):
                                        tn.wait_ge(S.h, s0 + kt + 1)
                                        if kt >= 4 * js:
                                            tn.wait_ge(V.h, mask_tok[kt])
                                        if kt == 0 and prev is not None:
                                            tn.wait_ge(V.h, prev)
                                        ins = tn.matmul(
                                            pav[js][0:65, :],
                                            Vp[:, kt * 260 + h8 * 65:
                                               kt * 260 + h8 * 65 + 65],
                                            exp_sb[:, kt * 512:(kt + 1) * 512],
                                            start=(kt == 0),
                                            stop=(kt == nk - 1))
                                        T.inc(ins)
                                block.tensor(fB)
                                av_tok[js] = T.n
                                gidx += 1

                            def fvdB(v, av3=av_tok[3], sd_prev=sd_last):
                                v.wait_ge(T.h, av3)
                                if sd_prev is not None:
                                    v.wait_ge(S.h, sd_prev)
                                ins = None
                                for js in range(4):
                                    ins = v.tensor_copy(
                                        dn4[0:1, js * 512:(js + 1) * 512],
                                        pav[js][64:65, :])
                                V.inc(ins)
                            block.vector(fvdB)
                            dn_tok = V.n

                            def fsdB(sc, dn_tok=dn_tok):
                                sc.wait_ge(V.h, dn_tok)
                                sc.activation(dn4[:], dn4[:], ACT.Ln)
                                sc.drain()
                                ins = sc.activation(dn4[:], dn4[:],
                                                    ACT.Exp, scale=-1.0)
                                S.inc(ins)
                            block.scalar(fsdB)
                            sd_last = S.n
                            rd_tok = S.n

                            def fvd2B(v, rd_tok=rd_tok):
                                v.wait_ge(S.h, rd_tok)
                                ins = v.tensor_copy(rd4[:], dn4[:])
                                V.inc(ins)
                            block.vector(fvd2B)
                            rdb_tok = V.n

                            for js in range(4):
                                def ftrd(tn, js=js, rdb_tok=rdb_tok,
                                         srd_prev=srd_last):
                                    tn.wait_ge(V.h, rdb_tok)
                                    if srd_prev is not None:
                                        tn.wait_ge(S.h, srd_prev)
                                    ins = tn.matmul(
                                        prd[:], ones_row[0:1, 0:64],
                                        rd4[0:1, js * 512:(js + 1) * 512],
                                        start=True, stop=True)
                                    T.inc(ins)
                                block.tensor(ftrd)
                                prd_tok = T.n

                                def fsrd(sc, js=js, prd_tok=prd_tok):
                                    sc.wait_ge(T.h, prd_tok)
                                    ins = sc.copy(
                                        prd_sb4[:, js * 512:(js + 1) * 512],
                                        prd[:])
                                    S.inc(ins)
                                block.scalar(fsrd)
                                srd_last = S.n
                                prds_tok = S.n

                                def fvao(v, cc=cc, row0=row0, js=js,
                                         prds_tok=prds_tok):
                                    v.wait_ge(S.h, prds_tok)
                                    ins = v.tensor_mul(
                                        attnout[row0:row0 + 64,
                                                cc * TOK + js * 512:
                                                cc * TOK + js * 512 + 512],
                                        pav[js][0:64, :],
                                        prd_sb4[:, js * 512:(js + 1) * 512])
                                    V.inc(ins)
                                block.vector(fvao)
                                ao_ev[h8 * 4 + js] = V.n
                            ao_last = V.n
                ao_tok = V.n
                p7_s_end = S.n
            dtap("attnout", [128, NCH * TOK], BF16)
            dump("attnout", attnout[:], V.h, ao_tok)

            with ExitStack() as p8:
                wo_b = sbuf("wo_b", [128, NCH * 1024], BF16, p8)
                borow = sbuf("p8_bo", [1, D], F32, p8)
                borow_b = sbuf("p8_bob", [1, D], BF16, p8)
                boh = sbuf("p8_boh", [128, D], F32, p8)
                g1l = [sbuf(f"p8_g1{i}", [128, D], BF16, p8) for i in range(2)]
                xo = [sbuf(f"p8_xo{i}", [128, D], F32, p8) for i in range(2)]
                t1s = [sbuf(f"p8_t1{i}", [128, D], F32, p8) for i in range(2)]
                pso = [psum(f"p8_ps{i}", [128, 1024], F32, p8) for i in range(2)]
                pbo = psum("p8_pbo", [128, 512], F32, p8)

                WOL = Tok("sem_wol")
                BOL = Tok("sem_bol")

                def fgw(sy, te=T.n, tv=V.n, ts=S.n):
                    sy.wait_ge(PC.h, pc["wo"])
                    sy.wait_ge(T.h, te)
                    sy.wait_ge(V.h, tv)
                    sy.wait_ge(S.h, ts)
                    sy.dma_start(
                        out=wo_b[:].rearrange("p (c i) -> p c i", c=NCH),
                        in_=woB.rearrange("(c p) i -> p c i", p=128)
                    ).then_inc(WOL.h, 16)
                    WOL.n += 16
                block.sync(fgw)

                def fgb2(g, te=T.n):
                    g.wait_ge(T.h, te)
                    g.dma_start(out=borow[:], in_=bo_in).then_inc(BOL.h, 16)
                    BOL.n += 16
                block.gpsimd(fgb2)

                def fvbo(v):
                    v.wait_ge(BOL.h, 16)
                    ins = v.tensor_copy(borow_b[:], borow[:])
                    V.inc(ins)
                block.vector(fvbo)
                bob_tok = V.n

                boev = {}
                for nn in range(2):
                    def ftbo(tn, nn=nn):
                        if nn == 0:
                            tn.wait_ge(V.h, bob_tok)
                        else:
                            tn.wait_ge(S.h, boev[0])
                        ins = tn.matmul(pbo[:], ones_row[:],
                                        borow_b[:, nn * 512:(nn + 1) * 512],
                                        start=True, stop=True)
                        T.inc(ins)
                    block.tensor(ftbo)
                    tbo = T.n

                    def fsbo(s, nn=nn, tbo=tbo):
                        s.wait_ge(T.h, tbo)
                        ins = s.activation(boh[:, nn * 512:(nn + 1) * 512],
                                           pbo[:], ACT.Copy, scale=0.5)
                        S.inc(ins)
                    block.scalar(fsbo)
                    boev[nn] = S.n
                boh_tok = boev[1]

                fuse_v = {}
                do_tok = {}
                prev_rel = {}
                for t in range(NT):
                    q = t % 2

                    def fsyg(sy, t=t, q=q, pr=prev_rel.get(t - 2),
                             tv=V.n, ts=S.n, te=T.n):
                        if t < 2:
                            sy.wait_ge(V.h, tv)
                            sy.wait_ge(S.h, ts)
                            sy.wait_ge(T.h, te)
                        if t == 0:
                            sy.wait_ge(*g1_done[0])
                            sy.wait_ge(*g1_done[1])
                        if t >= 2:
                            sy.wait_ge(V.h, fuse_v[t - 2])
                            sy.wait_ge(*pr)
                        sy.dma_start(out=g1l[q][:],
                                     in_=g1_spill[t * 128:(t + 1) * 128, :]
                                     ).then_inc(DXS[q].h, 16)
                        DXS[q].n += 16
                    block.sync(fsyg)
                    gl_tok = (DXS[q].h, DXS[q].n)
                    prev_rel[t] = gl_tok

                    def fto(tn, t=t, q=q):
                        if t == 0:
                            tn.wait_ge(WOL.h, 16)
                            tn.wait_ge(V.h, ao_tok)
                            tn.wait_ge(S.h, p7_s_end)
                        if t >= 2:
                            tn.wait_ge(V.h, fuse_v[t - 2])
                        ins = None
                        for c in range(NCH):
                            for nn in range(2):
                                ins = tn.matmul(
                                    pso[q][:, nn * 512:(nn + 1) * 512],
                                    attnout[:, c * TOK + t * 128:
                                            c * TOK + t * 128 + 128],
                                    wo_b[:, c * 1024 + nn * 512:
                                         c * 1024 + nn * 512 + 512],
                                    start=(c == 0), stop=(c == NCH - 1))
                        T.inc(ins)
                    block.tensor(fto)
                    ttok = T.n

                    def fvo2(v, t=t, q=q, ttok=ttok, gl_tok=gl_tok):
                        v.wait_ge(T.h, ttok)
                        v.wait_ge(*gl_tok)
                        if t == 0:
                            v.wait_ge(S.h, boh_tok)
                        if t >= 2:
                            v.wait_ge(DO.h, do_tok[t - 2])
                        v.scalar_tensor_tensor(t1s[q][:], pso[q][:], 0.5,
                                               boh[:], ALU.mult, ALU.add)
                        v.drain()
                        v.tensor_mul(t1s[q][:], t1s[q][:], g1l[q][:])
                        v.drain()
                        ins = v.tensor_add(xo[q][:],
                                           x_new[:, t * D:(t + 1) * D],
                                           t1s[q][:])
                        V.inc(ins)
                    block.vector(fvo2)
                    fuse_v[t] = V.n
                    vtok = V.n

                    def fsyo(sy, t=t, q=q, vtok=vtok, dprev=DO.n):
                        sy.wait_ge(DO.h, dprev)
                        sy.wait_ge(V.h, vtok)
                        sy.dma_start(out=x_out[t * 128:(t + 1) * 128, :],
                                     in_=xo[q][:]).then_inc(DO.h, 16)
                        DO.n += 16
                    block.sync(fsyo)
                    do_tok[t] = DO.n

        def ffin(sy):
            sy.wait_ge(DO.h, DO.n)
        block.sync(ffin)

    return nc, dbg_out



def _stage_inputs(inputs):
    f = np.ascontiguousarray
    i32 = np.float32
    common = {}
    common["ident_in"] = np.eye(128, dtype=i32)
    cm = np.zeros((128, 4 * 512), dtype=i32)
    kk = np.arange(128)[:, None]
    qq = np.arange(512)[None, :]
    for ri, r in enumerate((0, 128, 256, 384)):
        cm[:, ri * 512:(ri + 1) * 512] = ((kk + r) <= qq).astype(i32)
    common["causal_in"] = cm
    common["ones_in"] = np.ones((1, 128), dtype=i32)
    pat = np.zeros((16, NCH * 128), dtype=i32)
    for c in range(NCH):
        for p in range(128):
            pat[2 * c + p // 64, c * 128 + p] = 1.0
    common["pat16_in"] = pat
    ld = np.asarray(inputs["L0_attn_log_decay"], dtype=i32)
    ldc = np.zeros((128, NCH), dtype=i32)
    for c in range(NCH):
        for p in range(128):
            ldc[p, c] = ld[2 * c + p // 64]
    common["ld_cols"] = ldc
    q_ = np.asarray(inputs["L0_attn_q"], dtype=i32).reshape(16, 64).T
    k_ = np.asarray(inputs["L0_attn_k"], dtype=i32).reshape(16, 64).T
    common["qk_in"] = f(np.concatenate([q_, k_], axis=1))
    common["nw_cols"] = f(np.asarray(
        inputs["L0_attn_norm_weight"], dtype=i32).reshape(NCH, 128).T)
    common["owT"] = f(np.asarray(inputs["L0_attn_o_w"], dtype=i32).T)
    common["wgsT"] = f(np.asarray(inputs["L0_ffn_Wg_s"], dtype=i32).T)
    common["wusT"] = f(np.asarray(inputs["L0_ffn_Wu_s"], dtype=i32).T)
    common["wdsT"] = f(np.asarray(inputs["L0_ffn_Wd_s"], dtype=i32).T)
    common["wgT"] = f(np.asarray(inputs["L0_ffn_Wg"], dtype=i32).transpose(0, 2, 1))
    common["wuT"] = f(np.asarray(inputs["L0_ffn_Wu"], dtype=i32).transpose(0, 2, 1))
    common["wdT"] = f(np.asarray(inputs["L0_ffn_Wd"], dtype=i32).transpose(0, 2, 1))
    common["wrT"] = f(np.asarray(inputs["L0_ffn_Wr"], dtype=i32).T)
    common["rb_in"] = f(np.asarray(inputs["L0_ffn_rb"], dtype=i32).reshape(1, NE0))
    common["wqT"] = f(np.asarray(inputs["L1_attn_Wq"], dtype=i32).T)
    common["wkT"] = f(np.asarray(inputs["L1_attn_Wk"], dtype=i32).T)
    common["wvT"] = f(np.asarray(inputs["L1_attn_Wv"], dtype=i32).T)
    common["woT"] = f(np.asarray(inputs["L1_attn_Wo"], dtype=i32).T)
    common["bq_in"] = f(np.asarray(inputs["L1_attn_bq"], dtype=i32).reshape(1, D))
    common["bk_in"] = f(np.asarray(inputs["L1_attn_bk"], dtype=i32).reshape(1, D))
    common["bv_in"] = f(np.asarray(inputs["L1_attn_bv"], dtype=i32).reshape(1, D))
    common["bo_in"] = f(np.asarray(inputs["L1_attn_bo"], dtype=i32).reshape(1, D))
    common["se_in"] = f(np.asarray(
        inputs["L1_ffn_slot_embeds"], dtype=i32).reshape(E1 * S1, D))
    common["ng_in"] = f(np.asarray(inputs["L1_ffn_norm_g"], dtype=i32).reshape(1, D))
    common["nb_in"] = f(np.asarray(inputs["L1_ffn_norm_b"], dtype=i32).reshape(1, D))
    common["sg_in"] = f(np.asarray(inputs["L1_ffn_slot_g"], dtype=i32).reshape(1, D))
    common["sb_in"] = f(np.asarray(inputs["L1_ffn_slot_b"], dtype=i32).reshape(1, D))
    common["sc1_in"] = f(np.asarray(inputs["L1_ffn_scale"], dtype=i32).reshape(1, 1))
    common["wg1"] = f(np.asarray(inputs["L1_ffn_Wg"], dtype=i32))
    common["wu1"] = f(np.asarray(inputs["L1_ffn_Wu"], dtype=i32))
    common["wd1"] = f(np.asarray(inputs["L1_ffn_Wd"], dtype=i32))
    common["bg1"] = f(np.asarray(inputs["L1_ffn_bg"], dtype=i32).reshape(1, E1 * F1))
    common["bu1"] = f(np.asarray(inputs["L1_ffn_bu"], dtype=i32).reshape(1, E1 * F1))
    common["bd1"] = f(np.asarray(inputs["L1_ffn_bd"], dtype=i32).reshape(1, E1 * D))

    x = np.asarray(inputs["L0_attn_x"], dtype=i32)
    in_maps = []
    for b in range(8):
        m = dict(common)
        m["x_in"] = f(x[b])
        in_maps.append(m)
    return in_maps


_PROG_CACHE = {}


def _get_prog(debug=()):
    key = tuple(sorted(debug))
    if key not in _PROG_CACHE:
        _PROG_CACHE[key] = build_core_program(debug=key)
    return _PROG_CACHE[key]


def kernel(**inputs):
    nc, _ = _get_prog()
    in_maps = _stage_inputs(inputs)
    res = run_bass_kernel_spmd(nc, in_maps, core_ids=list(range(8)))
    out = np.stack([np.asarray(r["x_out"]) for r in res.results], axis=0)
    return out.astype(np.float32)


# version 08 (final)
# speedup vs baseline: 1.0987x; 1.0987x over previous
import numpy as np
from contextlib import ExitStack

import concourse.bass as bass
import concourse.mybir as mybir
from concourse.bass_utils import run_bass_kernel_spmd

F32 = mybir.dt.float32
BF16 = mybir.dt.bfloat16
ALU = mybir.AluOpType
ACT = mybir.ActivationFunctionType
AX = mybir.AxisListType.X

TOK = 2048
D = 1024
NT = TOK // 128
NCH = D // 128
NH = 16
DH = 64
IFF = 2048
NE0 = 8
E1, S1, F1 = 16, 64, 2048
SCALE1 = 1.0 / np.sqrt(DH * 2.0)
TEMP1 = 1.5
LN_EPS = 1e-5
RMS_EPS = 1e-6
FFN1_EPS = 1e-5
QK_BIAS = float(np.log(SCALE1 / TEMP1))


def build_core_program(debug=()):
    nc = bass.Bass("TRN2", target_bir_lowering=False)
    dbg = set(debug)
    dbg_out = {}

    def din(name, shape):
        return nc.dram_tensor(name, list(shape), F32, kind="ExternalInput").ap()

    x_in = din("x_in", [TOK, D])
    ident_in = din("ident_in", [128, 128])
    causal_in = din("causal_in", [128, 4 * 512])
    ones_in = din("ones_in", [1, 128])
    pat16_in = din("pat16_in", [16, NCH * 128])
    ld_cols_in = din("ld_cols", [128, NCH])
    qk_in = din("qk_in", [64, 32])
    nw_cols_in = din("nw_cols", [128, NCH])
    owT_in = din("owT", [D, D])
    wgsT_in = din("wgsT", [D, IFF])
    wusT_in = din("wusT", [D, IFF])
    wdsT_in = din("wdsT", [IFF, D])
    wgT_in = din("wgT", [NE0, D, IFF])
    wuT_in = din("wuT", [NE0, D, IFF])
    wdT_in = din("wdT", [NE0, IFF, D])
    wrT_in = din("wrT", [D, NE0])
    rb_in = din("rb_in", [1, NE0])
    wqT_in = din("wqT", [D, D])
    wkT_in = din("wkT", [D, D])
    wvT_in = din("wvT", [D, D])
    woT_in = din("woT", [D, D])
    bq_in = din("bq_in", [1, D])
    bk_in = din("bk_in", [1, D])
    bv_in = din("bv_in", [1, D])
    bo_in = din("bo_in", [1, D])
    se_in = din("se_in", [E1 * S1, D])
    ng_in = din("ng_in", [1, D])
    nb_in = din("nb_in", [1, D])
    sg_in = din("sg_in", [1, D])
    sb_in = din("sb_in", [1, D])
    sc1_in = din("sc1_in", [1, 1])
    wg1_in = din("wg1", [E1, D, F1])
    wu1_in = din("wu1", [E1, D, F1])
    wd1_in = din("wd1", [E1, F1, D])
    bg1_in = din("bg1", [1, E1 * F1])
    bu1_in = din("bu1", [1, E1 * F1])
    bd1_in = din("bd1", [1, E1 * D])

    x_out = nc.dram_tensor("x_out", [TOK, D], F32, kind="ExternalOutput").ap()
    def dbf(name, shape):
        return nc.dram_tensor(name, list(shape), BF16, kind="Internal").ap()
    wgsB = dbf("wgsB", [D, IFF])
    wusB = dbf("wusB", [D, IFF])
    wdsB = dbf("wdsB", [IFF, D])
    wgB = dbf("wgB", [NE0, D, IFF])
    wuB = dbf("wuB", [NE0, D, IFF])
    wdB = dbf("wdB", [NE0, IFF, D])
    owB = dbf("owB", [D, D])
    wqB = dbf("wqB", [D, D])
    wkB = dbf("wkB", [D, D])
    wvB = dbf("wvB", [D, D])
    woB = dbf("woB", [D, D])
    wg1B = dbf("wg1B", [E1, D, F1])
    wu1B = dbf("wu1B", [E1, D, F1])
    wd1B = dbf("wd1B", [E1, F1, D])
    g1_spill = nc.dram_tensor("g1_spill", [TOK, D], BF16, kind="Internal").ap()
    comb_sp = nc.dram_tensor("comb_sp", [128, NT * 1024], BF16, kind="Internal").ap()

    def dtap(name, shape, dtype=F32):
        if name in dbg:
            dbg_out[name] = nc.dram_tensor(
                "dbg_" + name, list(shape), dtype, kind="ExternalOutput").ap()
        return dbg_out.get(name)

    es = ExitStack()
    with es:
        _uniq = [0]

        def _un(name):
            _uniq[0] += 1
            return f"{name}_{_uniq[0]}"

        def sbuf(name, shape, dtype, ph=None):
            return (ph or es).enter_context(
                nc.sbuf_tensor(_un(name), shape, dtype))

        def psum(name, shape, dtype=F32, ph=None):
            return (ph or es).enter_context(
                nc.psum_tensor(_un(name), shape, dtype))

        class Tok:
            def __init__(self, name):
                self.h = es.enter_context(nc.semaphore(name))
                self.n = 0

            def inc(self, ins, k=1):
                ins.then_inc(self.h, k)
                self.n += k
                return self.n

        ident_b = sbuf("ident_b", [128, 128], BF16)
        ones_row = sbuf("ones_row", [1, 128], BF16)
        pat16_b = sbuf("pat16_b", [16, NCH * 128], BF16)
        ld_cols = sbuf("ld_cols_s", [128, NCH], F32)
        qk_b = sbuf("qk_b", [64, 32], BF16)
        nw_cols = sbuf("nw_cols_s", [128, NCH], F32)
        rb_row = sbuf("rb_row", [1, NE0], BF16)
        r_cols = sbuf("r_cols", [128, NT], F32)
        nbias_cols = sbuf("nbias_cols", [128, NT], F32)
        v_cols = sbuf("v_cols", [128, NT], F32)
        rho_cols = sbuf("rho_cols", [128, NT], F32)
        nb2_cols = sbuf("nb2_cols", [128, NT], F32)
        fmbuf = sbuf("fmbuf", [128, NCH * TOK], BF16)
        xacc = sbuf("xacc", [128, NT * D], F32)
        acc = xacc
        x_new = xacc

        constv = sbuf("constv", [128, 4], F32)

        T = Tok("sem_t")
        V = Tok("sem_v")
        S = Tok("sem_s")
        DO = Tok("sem_do")
        DC = Tok("sem_dc")
        DXS = [Tok(f"sem_dx{i}") for i in range(2)]
        DWS = [Tok(f"sem_dw{i}") for i in range(4)]

        block = es.enter_context(nc.Block())

        def p0_v(v):
            v.memset(constv[:, 0:1], LN_EPS)
            v.memset(constv[:, 1:2], RMS_EPS)
            v.memset(constv[:, 2:3], QK_BIAS)
            ins = v.drain()
            V.inc(ins)
        block.vector(p0_v)
        nc.const_aps.register(F32, LN_EPS, constv[:, 0:1])
        nc.const_aps.register(F32, RMS_EPS, constv[:, 1:2])
        nc.const_aps.register(F32, QK_BIAS, constv[:, 2:3])

        wr_b_g = sbuf("wr_b", [128, NCH * NE0], BF16)

        def p0_g(g):
            for dst, src in ((ident_b, ident_in),
                             (ones_row, ones_in), (pat16_b, pat16_in),
                             (qk_b, qk_in), (rb_row, rb_in),
                             (ld_cols, ld_cols_in), (nw_cols, nw_cols_in)):
                g.dma_start(out=dst[:], in_=src).then_inc(DC.h, 16)
                DC.n += 16
            g.dma_start(
                out=wr_b_g[:].rearrange("p (c e) -> p c e", c=NCH),
                in_=wrT_in.rearrange("(c p) e -> p c e", p=128)
            ).then_inc(DC.h, 16)
            DC.n += 16
        block.gpsimd(p0_g)
        P0_DC = DC.n

        PC = Tok("sem_pc")
        pc = {}

        def p0_pc(g):
            plan = [("wgs", wgsB, wgsT_in), ("wus", wusB, wusT_in),
                    ("wds", wdsB, wdsT_in)]
            for e in range(NE0):
                plan += [(f"wg{e}", wgB[e], wgT_in[e]),
                         (f"wu{e}", wuB[e], wuT_in[e]),
                         (f"wd{e}", wdB[e], wdT_in[e])]
            plan += [("ow", owB, owT_in), ("wq", wqB, wqT_in),
                     ("wk", wkB, wkT_in), ("wv", wvB, wvT_in)]
            for e in range(E1):
                plan += [(f"g1{e}", wg1B[e], wg1_in[e]),
                         (f"u1{e}", wu1B[e], wu1_in[e]),
                         (f"d1{e}", wd1B[e], wd1_in[e])]
            plan += [("wo", woB, woT_in)]
            for i, (key, dst, src) in enumerate(plan):
                if i:
                    g.wait_ge(PC.h, PC.n)
                g.dma_start(out=dst, in_=src).then_inc(PC.h, 16)
                PC.n += 16
                pc[key] = PC.n
        block.gpsimd(p0_pc)

        def hbm_loader(ph, hbm_ap, name, init_guard=()):
            xt = [sbuf(f"{name}{i}", [128, D], F32, ph) for i in range(2)]
            consumed = {}

            def load(t):
                q = t % 2

                def fsy(sy, t=t, q=q):
                    if t >= 2:
                        sem, val = consumed[t - 2]
                        sy.wait_ge(sem, val)
                    else:
                        for gw in init_guard:
                            sy.wait_ge(*gw)
                    sy.dma_start(out=xt[q][:],
                                 in_=hbm_ap[t * 128:(t + 1) * 128, :]
                                 ).then_inc(DXS[q].h, 16)
                    DXS[q].n += 16
                block.sync(fsy)
                return xt[q][:], (DXS[q].h, DXS[q].n)

            def mark(t, sem, val):
                consumed[t] = (sem, val)
            return load, mark

        def stats_pass(ph, src_of, mark, first_wait=()):
            c8 = sbuf("st_c8", [128, 8], F32, ph)
            x2 = sbuf("st_x2", [128, D], F32, ph)
            for t in range(NT):
                src, guard = src_of(t)

                def fv(v, t=t, src=src, guard=guard):
                    if guard is not None:
                        v.wait_ge(*guard)
                    if t == 0:
                        for fw in first_wait:
                            v.wait_ge(*fw)
                    v.tensor_mul(x2[:], src, src)
                    v.drain()
                    v.reduce_sum(c8[:, 0:1], src, axis=AX)
                    v.reduce_sum(c8[:, 1:2], x2[:], axis=AX)
                    v.drain()
                    v.tensor_scalar(c8[:, 2:3], c8[:, 0:1], 1.0 / D, None, ALU.mult)
                    v.tensor_scalar(c8[:, 4:5], c8[:, 1:2], 1.0 / D, None, ALU.mult)
                    v.drain()
                    v.tensor_mul(c8[:, 3:4], c8[:, 2:3], c8[:, 2:3])
                    v.drain()
                    ins = v.tensor_sub(v_cols[:, t:t + 1], c8[:, 4:5], c8[:, 3:4])
                    V.inc(ins)
                block.vector(fv)
                mark(t, V.h, V.n)
                vtok = V.n

                def fs(s, t=t, vtok=vtok):
                    s.wait_ge(V.h, vtok)
                    s.activation(c8[:, 5:6], v_cols[:, t:t + 1], ACT.Ln,
                                 bias=LN_EPS)
                    s.drain()
                    ins = s.activation(r_cols[:, t:t + 1], c8[:, 5:6],
                                       ACT.Exp, scale=-0.5)
                    S.inc(ins)
                block.scalar(fs)
                stok = S.n

                def fv2(v, t=t, stok=stok):
                    v.wait_ge(S.h, stok)
                    ins = v.scalar_tensor_tensor(
                        nbias_cols[:, t:t + 1], c8[:, 2:3], -1.0,
                        r_cols[:, t:t + 1], ALU.mult, ALU.mult)
                    V.inc(ins)
                block.vector(fv2)

        def fm_build(ph, src_of, mark, sc_cols, bs_cols, out_fm, gb=None,
                     first_wait=()):
            tm = [sbuf(f"fb_tm{i}", [128, D], BF16, ph) for i in range(2)]
            a32 = [sbuf(f"fb_a{i}", [128, D], F32, ph) for i in range(2)] \
                if gb else [None, None]
            ptr = [psum(f"fb_p{i}", [128, 1024], BF16, ph) for i in range(2)]
            ev = {}
            tmfree = {}
            for t in range(NT):
                q = t % 2
                src, guard = src_of(t)

                def fv0(v, t=t, q=q, src=src, guard=guard):
                    if guard is not None:
                        v.wait_ge(*guard)
                    if t == 0:
                        for fw in first_wait:
                            v.wait_ge(*fw)
                    if t >= 2:
                        v.wait_ge(T.h, tmfree[t - 2])
                    if gb:
                        v.tensor_scalar(a32[q][:], src,
                                        sc_cols[:, t:t + 1],
                                        bs_cols[:, t:t + 1],
                                        ALU.mult, ALU.add)
                        v.drain()
                        v.tensor_mul(a32[q][:], a32[q][:], gb[0][:])
                        v.drain()
                        ins = v.tensor_add(tm[q][:], a32[q][:], gb[1][:])
                    else:
                        ins = v.tensor_scalar(tm[q][:], src,
                                              sc_cols[:, t:t + 1],
                                              bs_cols[:, t:t + 1],
                                              ALU.mult, ALU.add)
                    V.inc(ins)
                block.vector(fv0)
                mark(t, V.h, V.n)
                wsem, wval = V.h, V.n

                def ft(tn, t=t, q=q, wsem=wsem, wval=wval):
                    tn.wait_ge(wsem, wval)
                    if t == 0:
                        for fw in first_wait:
                            tn.wait_ge(*fw)
                    if t >= 2:
                        tn.wait_ge(V.h, ev[t - 2])
                    for c in range(NCH):
                        ins = tn.transpose(ptr[q][:, c * 128:(c + 1) * 128],
                                           tm[q][:, c * 128:(c + 1) * 128],
                                           ident_b[:])
                    T.inc(ins)
                block.tensor(ft)
                tmfree[t] = T.n
                ttok = T.n

                def fv2(v, t=t, q=q, ttok=ttok):
                    v.wait_ge(T.h, ttok)
                    dst = out_fm[:].rearrange("p (c n) -> p c n", c=NCH)[
                        :, :, t * 128:(t + 1) * 128]
                    ins = v.tensor_copy(
                        dst, ptr[q][:].rearrange("p (c i) -> p c i", c=8))
                    V.inc(ins)
                block.vector(fv2)
                ev[t] = V.n
            return V.n

        def dump(name, buf, wait_sem, wait_val):
            if name not in dbg_out:
                return

            def fsy(sy, dprev=DO.n):
                sy.wait_ge(DO.h, dprev)
                sy.wait_ge(wait_sem, wait_val)
                sy.dma_start(out=dbg_out[name], in_=buf).then_inc(DO.h, 16)
                DO.n += 16
            block.sync(fsy)

            def fvbar(v, dn=DO.n):
                v.wait_ge(DO.h, dn)
                ins = v.drain()
                V.inc(ins)
            block.vector(fvbar)

            def fsbar(sc, dn=DO.n):
                sc.wait_ge(DO.h, dn)
                ins = sc.drain()
                S.inc(ins)
            block.scalar(fsbar)

        with ExitStack() as l0:
            cw_cols = sbuf("cw_cols", [128, NT * NE0], F32, l0)

            with ExitStack() as ph:
                load, mark = hbm_loader(ph, x_in, "p1a")
                stats_pass(ph, load, mark, first_wait=[(DC.h, P0_DC)])
            with ExitStack() as ph:
                load, mark = hbm_loader(ph, x_in, "p1b",
                                        init_guard=[(V.h, V.n)])
                p1_fm_tok = fm_build(ph, load, mark, r_cols, nbias_cols, fmbuf,
                                     first_wait=[(DC.h, P0_DC)])
            dtap("xn0_fm", [128, NCH * TOK], BF16)
            dump("xn0_fm", fmbuf[:], V.h, p1_fm_tok)
            dtap("r0", [128, NT])
            dump("r0", r_cols[:], V.h, V.n)

            with ExitStack() as ph:
                wr_b = wr_b_g
                wr_tok = P0_DC

                probs = [sbuf(f"p2_pr{i}", [128, NE0], F32, ph) for i in range(2)]
                mx8 = sbuf("p2_mx", [128, 8], F32, ph)
                zap = sbuf("p2_zap", [128, NE0], F32, ph)
                pr = [psum(f"p2_ps{i}", [128, NE0], F32, ph) for i in range(2)]
                evtok = {}
                sgtok = {}
                for t in range(NT):
                    q = t % 2

                    def ft(tn, t=t, q=q):
                        if t == 0:
                            tn.wait_ge(DC.h, wr_tok)
                            tn.wait_ge(V.h, p1_fm_tok)
                        if t >= 2:
                            tn.wait_ge(S.h, sgtok[t - 2])
                        tn.matmul(pr[q][:], ones_row[:], rb_row[:],
                                  start=True, stop=False)
                        ins = None
                        for c in range(NCH):
                            ins = tn.matmul(
                                pr[q][:],
                                fmbuf[:, c * TOK + t * 128:c * TOK + t * 128 + 128],
                                wr_b[:, c * NE0:(c + 1) * NE0],
                                start=False, stop=(c == NCH - 1))
                        T.inc(ins)
                    block.tensor(ft)
                    ttok = T.n

                    def fs(s, t=t, q=q, ttok=ttok):
                        s.wait_ge(T.h, ttok)
                        if t >= 2:
                            s.wait_ge(V.h, evtok[t - 2])
                        ins = s.activation(probs[q][:], pr[q][:], ACT.Sigmoid)
                        S.inc(ins)
                    block.scalar(fs)
                    sgtok[t] = S.n
                    stok = S.n

                    def fv(v, t=t, q=q, stok=stok):
                        v.wait_ge(S.h, stok)
                        v.max(out=mx8[:], in_=probs[q][:])
                        v.drain()
                        v.memset(mx8[:, 2:], 0.0)
                        v.drain()
                        v.match_replace(out=zap[:], in_to_replace=mx8[:],
                                        in_values=probs[q][:], imm_value=0.0)
                        v.drain()
                        ins = v.tensor_sub(cw_cols[:, t * NE0:(t + 1) * NE0],
                                           probs[q][:], zap[:])
                        V.inc(ins)
                    block.vector(fv)
                    evtok[t] = V.n
                p2_cw_tok = V.n
            dtap("cw", [128, NT * NE0])
            dump("cw", cw_cols[:], V.h, p2_cw_tok)

            with ExitStack() as ph:
                wg_s = sbuf("wg_s", [128, NCH * 1024], BF16, ph)
                wu_s = sbuf("wu_s", [128, NCH * 1024], BF16, ph)
                wd_s = sbuf("wd_s", [128, NCH * 1024], BF16, ph)
                h_b = sbuf("h_b", [128, NCH * TOK], BF16, ph)
                tsil = [sbuf(f"p3_sl{i}", [128, 512], BF16, ph) for i in range(2)]
                pg = [psum(f"p3_pg{i}", [128, 512], F32, ph) for i in range(2)]
                pu = [psum(f"p3_pu{i}", [128, 512], F32, ph) for i in range(2)]
                pd = [psum(f"p3_pd{i}", [128, 1024], F32, ph) for i in range(2)]
                FG, FU, FD = Tok("sem_fg"), Tok("sem_fu"), Tok("sem_fd")

                steps = [(e, hf) for e in range(-1, NE0) for hf in range(2)]
                NS = len(steps)

                def emit_wdma(si):
                    if si >= NS:
                        return
                    e, hf = steps[si]
                    pck = pc["wds"] if e == -1 else pc[f"wd{e}"]

                    def fg_(sy, si=si, e=e, hf=hf, te=T.n, tv=V.n, pck=pck):
                        sy.wait_ge(PC.h, pck)
                        if si >= 1:
                            sy.wait_ge(FG.h, si)
                            sy.wait_ge(FU.h, si)
                            sy.wait_ge(FD.h, si)
                        else:
                            sy.wait_ge(T.h, te)
                            sy.wait_ge(V.h, tv)
                        if e == -1:
                            gsrc = wgsB[:, hf * 1024:(hf + 1) * 1024]
                            usrc = wusB[:, hf * 1024:(hf + 1) * 1024]
                            dsrc = wdsB[hf * 1024:(hf + 1) * 1024, :]
                        else:
                            gsrc = wgB[e, :, hf * 1024:(hf + 1) * 1024]
                            usrc = wuB[e, :, hf * 1024:(hf + 1) * 1024]
                            dsrc = wdB[e, hf * 1024:(hf + 1) * 1024, :]
                        for k, (dst, src) in enumerate(
                                ((wg_s, gsrc), (wu_s, usrc), (wd_s, dsrc))):
                            sy.dma_start(
                                out=dst[:].rearrange("p (c i) -> p c i", c=NCH),
                                in_=src.rearrange("(c p) i -> p c i", p=128)
                            ).then_inc(DWS[k].h, 16)
                            DWS[k].n += 16
                    block.sync(fg_)
                emit_wdma(0)

                dn_ev = {}
                prev_silu_end = None
                prev_h_end = None
                for si, (e, hf) in enumerate(steps):
                    t0 = T.n
                    s0 = S.n
                    v0 = V.n

                    def ftn(tn, si=si, t0=t0, s0=s0, v0=v0,
                            pse=prev_silu_end, phe=prev_h_end):
                        tn.wait_ge(DWS[0].h, (si + 1) * 16)
                        tn.wait_ge(DWS[1].h, (si + 1) * 16)
                        if si == 0:
                            tn.wait_ge(V.h, p2_cw_tok)
                        for idx in range(32):
                            ic, n = divmod(idx, 4)
                            p = idx % 2
                            if idx >= 2:
                                tn.wait_ge(S.h, s0 + idx - 1)
                                tn.wait_ge(V.h, v0 + idx - 1)
                            elif pse is not None:
                                tn.wait_ge(S.h, pse)
                                tn.wait_ge(V.h, phe)
                            ins = None
                            for c in range(NCH):
                                ins = tn.matmul(
                                    pg[p][:],
                                    wg_s[:, c * 1024 + ic * 128:
                                         c * 1024 + ic * 128 + 128],
                                    fmbuf[:, c * TOK + n * 512:
                                          c * TOK + n * 512 + 512],
                                    start=(c == 0), stop=(c == NCH - 1))
                            T.inc(ins)
                            for c in range(NCH):
                                ins = tn.matmul(
                                    pu[p][:],
                                    wu_s[:, c * 1024 + ic * 128:
                                         c * 1024 + ic * 128 + 128],
                                    fmbuf[:, c * TOK + n * 512:
                                          c * TOK + n * 512 + 512],
                                    start=(c == 0), stop=(c == NCH - 1))
                            T.inc(ins)
                        tn.wait_ge(T.h, t0 + 64)
                        tn.nop().then_inc(FG.h, 1)
                        tn.nop().then_inc(FU.h, 1)
                    block.tensor(ftn)
                    FG.n += 1
                    FU.n += 1

                    def fsn(s, t0=t0, s0=s0, v0=v0, phe=prev_h_end):
                        for idx in range(32):
                            p = idx % 2
                            s.wait_ge(T.h, t0 + 2 * idx + 1)
                            if idx >= 2:
                                s.wait_ge(V.h, v0 + idx - 1)
                            elif phe is not None:
                                s.wait_ge(V.h, phe)
                            ins = s.activation(tsil[p][:], pg[p][:], ACT.Silu)
                            S.inc(ins)
                    block.scalar(fsn)

                    def fvn(v, t0=t0, s0=s0):
                        for idx in range(32):
                            ic, n = divmod(idx, 4)
                            p = idx % 2
                            v.wait_ge(T.h, t0 + 2 * idx + 2)
                            v.wait_ge(S.h, s0 + idx + 1)
                            ins = v.tensor_mul(
                                h_b[:, ic * TOK + n * 512:
                                    ic * TOK + n * 512 + 512],
                                tsil[p][:], pu[p][:])
                            V.inc(ins)
                    block.vector(fvn)
                    prev_silu_end = s0 + 32
                    prev_h_end = v0 + 32
                    vh_all = V.n
                    t1 = T.n

                    for t in range(NT):
                        p = t % 2

                        def ftd(tn, si=si, t=t, p=p, vh=vh_all):
                            if t == 0:
                                tn.wait_ge(DWS[2].h, (si + 1) * 16)
                                tn.wait_ge(V.h, vh)
                            if t >= 2:
                                sem, val = dn_ev[(si, t - 2)]
                                tn.wait_ge(sem, val)
                            elif si >= 1:
                                sem, val = dn_ev[(si - 1, NT - 2 + t)]
                                tn.wait_ge(sem, val)
                            ins = None
                            for ic in range(NCH):
                                for nn in range(2):
                                    ins = tn.matmul(
                                        pd[p][:, nn * 512:(nn + 1) * 512],
                                        h_b[:, ic * TOK + t * 128:
                                            ic * TOK + t * 128 + 128],
                                        wd_s[:, ic * 1024 + nn * 512:
                                             ic * 1024 + nn * 512 + 512],
                                        start=(ic == 0), stop=(ic == NCH - 1))
                            T.inc(ins)
                        block.tensor(ftd)
                        dtok = T.n
                        prev = dn_ev.get((si - 1, t))
                        if e == -1 and hf == 0:
                            def fse(s, t=t, p=p, dtok=dtok):
                                s.wait_ge(T.h, dtok)
                                ins = s.copy(acc[:, t * D:(t + 1) * D], pd[p][:])
                                S.inc(ins)
                            block.scalar(fse)
                            dn_ev[(si, t)] = (S.h, S.n)
                        else:
                            def fve(v, t=t, p=p, dtok=dtok, e=e, prev=prev):
                                v.wait_ge(T.h, dtok)
                                if prev is not None and prev[0] is S.h:
                                    v.wait_ge(*prev)
                                if e == -1:
                                    ins = v.tensor_add(
                                        acc[:, t * D:(t + 1) * D], pd[p][:],
                                        acc[:, t * D:(t + 1) * D])
                                else:
                                    ins = v.scalar_tensor_tensor(
                                        acc[:, t * D:(t + 1) * D], pd[p][:],
                                        cw_cols[:, t * NE0 + e:t * NE0 + e + 1],
                                        acc[:, t * D:(t + 1) * D],
                                        ALU.mult, ALU.add)
                                V.inc(ins)
                            block.vector(fve)
                            dn_ev[(si, t)] = (V.h, V.n)

                    def ftfree2(tn, tv=T.n):
                        tn.wait_ge(T.h, tv)
                        tn.nop().then_inc(FD.h, 1)
                    block.tensor(ftfree2)
                    FD.n += 1

                    emit_wdma(si + 1)
                p3_acc_tok = dn_ev[(NS - 1, NT - 1)]
                p3_t_end = T.n
                p3_s_end = S.n
                p3_v_end = V.n
            dtap("acc0", [128, NT * D])
            dump("acc0", acc[:], *p3_acc_tok)

            with ExitStack() as ph:
                lam_cols = sbuf("lam_cols", [128, NCH], F32, ph)
                c_cols = sbuf("c_cols", [128, NCH], F32, ph)
                c2_b = sbuf("c2_b", [128, NCH], BF16, ph)
                cnw = sbuf("cnw", [128, NCH], F32, ph)
                qkm = sbuf("qkm", [64, 16], BF16, ph)
                ones_col64 = sbuf("ones_c64", [64, 1], BF16, ph)
                c16_b = sbuf("c16_b", [16, 1], BF16, ph)
                scan_sc = sbuf("scan_sc", [128, TOK], BF16, ph)
                y2_sc = sbuf("y2_sc", [128, TOK], BF16, ph)
                rrow = sbuf("rrow", [1, TOK], F32, ph)
                rrow_b = sbuf("rrow_b", [1, TOK], BF16, ph)
                rm_cols = sbuf("rm_cols", [128, NT], F32, ph)
                ones11 = sbuf("ones11", [1, 1], BF16, ph)
                ow_b = sbuf("ow_b", [128, NCH * 1024], BF16, ph)
                silu_sc = [sbuf(f"p4_si{i}", [128, D], F32, ph) for i in range(2)]
                s_sc = [sbuf(f"p4_ss{i}", [128, D], F32, ph) for i in range(2)]

                OWT = Tok("sem_owt")

                def fg(sy):
                    sy.wait_ge(T.h, p3_t_end)
                    sy.wait_ge(V.h, p3_v_end)
                    sy.wait_ge(S.h, p3_s_end)
                    sy.wait_ge(PC.h, pc["ow"])
                    sy.dma_start(
                        out=ow_b[:].rearrange("p (c i) -> p c i", c=NCH),
                        in_=owB.rearrange("(c p) i -> p c i", p=128)
                    ).then_inc(OWT.h, 16)
                    OWT.n += 16
                block.sync(fg)

                def fsl(s):
                    s.wait_ge(DC.h, P0_DC)
                    ins = s.activation(lam_cols[:], ld_cols[:], ACT.Sigmoid)
                    S.inc(ins)
                block.scalar(fsl)
                lam_tok = S.n

                def fvq(v):
                    v.wait_ge(DC.h, P0_DC)
                    v.wait_ge(DXS[0].h, 32)
                    v.memset(ones_col64[:], 1.0)
                    v.memset(ones11[:], 1.0)
                    v.drain()
                    ins = v.tensor_mul(qkm[:], qk_b[:, 0:16], qk_b[:, 16:32])
                    V.inc(ins)
                block.vector(fvq)
                qkm_tok = V.n

                with ExitStack() as pp:
                    pc16 = psum("p4_pc16", [16, 1], F32, pp)
                    pcc = psum("p4_pcc", [128, NCH], F32, pp)
                    pss = psum("p4_pss", [1, TOK], F32, pp)
                    prc = psum("p4_prc", [128, NT], F32, pp)

                    def ftc(tn):
                        tn.wait_ge(V.h, qkm_tok)
                        tn.wait_ge(S.h, p3_s_end)
                        tn.wait_ge(V.h, p3_v_end)
                        ins = tn.matmul(pc16[:], qkm[:], ones_col64[:],
                                        start=True, stop=True)
                        T.inc(ins)
                    block.tensor(ftc)
                    c16_tok = T.n

                    def fvc(v):
                        v.wait_ge(T.h, c16_tok)
                        ins = v.tensor_copy(c16_b[:], pc16[:])
                        V.inc(ins)
                    block.vector(fvc)
                    c16b_tok = V.n

                    def ftc2(tn):
                        tn.wait_ge(V.h, c16b_tok)
                        ins = None
                        for c in range(NCH):
                            ins = tn.matmul(pcc[:, c:c + 1],
                                            pat16_b[:, c * 128:(c + 1) * 128],
                                            c16_b[:], start=True, stop=True)
                        T.inc(ins)
                    block.tensor(ftc2)
                    ccols_tok = T.n

                    def fvc2(v):
                        v.wait_ge(T.h, ccols_tok)
                        v.tensor_copy(c_cols[:], pcc[:])
                        v.drain()
                        v.tensor_mul(cnw[:], c_cols[:], nw_cols[:])
                        ins = v.tensor_mul(c2_b[:], c_cols[:], c_cols[:])
                        V.inc(ins)
                    block.vector(fvc2)
                    cprep_tok = V.n

                    ss_mm = {}
                    for c in range(NCH):
                        def fvs(v, c=c):
                            if c == 0:
                                v.wait_ge(S.h, lam_tok)
                                v.wait_ge(V.h, cprep_tok)
                                v.wait_ge(T.h, p3_t_end)
                            else:
                                v.wait_ge(T.h, ss_mm[c - 1])
                            v.tensor_tensor_scan(
                                scan_sc[:],
                                lam_cols[:, c:c + 1].to_broadcast([128, TOK]),
                                fmbuf[:, c * TOK:(c + 1) * TOK],
                                0.0, ALU.mult, ALU.add)
                            v.drain()
                            v.tensor_copy(fmbuf[:, c * TOK:(c + 1) * TOK],
                                          scan_sc[:])
                            v.drain()
                            ins = v.tensor_mul(y2_sc[:],
                                               fmbuf[:, c * TOK:(c + 1) * TOK],
                                               fmbuf[:, c * TOK:(c + 1) * TOK])
                            V.inc(ins)
                        block.vector(fvs)
                        sctok = V.n

                        def fts(tn, c=c, sctok=sctok):
                            tn.wait_ge(V.h, sctok)
                            ins = None
                            for n in range(4):
                                ins = tn.matmul(pss[:, n * 512:(n + 1) * 512],
                                                c2_b[:, c:c + 1],
                                                y2_sc[:, n * 512:(n + 1) * 512],
                                                start=(c == 0), stop=(c == NCH - 1))
                            T.inc(ins)
                        block.tensor(fts)
                        ss_mm[c] = T.n

                    def fsr(s):
                        s.wait_ge(T.h, ss_mm[NCH - 1])
                        s.activation(rrow[:], pss[:], ACT.Ln, scale=1.0 / D,
                                     bias=RMS_EPS)
                        s.drain()
                        ins = s.activation(rrow[:], rrow[:], ACT.Exp, scale=-0.5)
                        S.inc(ins)
                    block.scalar(fsr)
                    rrow_tok = S.n

                    def fvr(v):
                        v.wait_ge(S.h, rrow_tok)
                        ins = v.tensor_copy(rrow_b[:], rrow[:])
                        V.inc(ins)
                    block.vector(fvr)
                    rrowb_tok = V.n

                    def ftr(tn):
                        tn.wait_ge(V.h, rrowb_tok)
                        ins = None
                        for t in range(NT):
                            ins = tn.matmul(prc[:, t:t + 1],
                                            rrow_b[:, t * 128:(t + 1) * 128],
                                            ones11[:], start=True, stop=True)
                        T.inc(ins)
                    block.tensor(ftr)
                    prc_tok = T.n

                    def fvr2(v):
                        v.wait_ge(T.h, prc_tok)
                        ins = v.tensor_copy(rm_cols[:], prc[:])
                        V.inc(ins)
                    block.vector(fvr2)
                    rm_tok = V.n

                def fvo(v):
                    v.wait_ge(OWT.h, 16)
                    v.drain()
                    ins = None
                    for c in range(NCH):
                        ins = v.tensor_scalar(ow_b[:, c * 1024:(c + 1) * 1024],
                                              ow_b[:, c * 1024:(c + 1) * 1024],
                                              cnw[:, c:c + 1], None, ALU.mult)
                    V.inc(ins)
                block.vector(fvo)
                owb_tok = V.n

                ps = [psum(f"p4_ps{i}", [128, 1024], F32, ph) for i in range(2)]
                load, mark = hbm_loader(ph, x_in, "p4x",
                                        init_guard=[(V.h, V.n), (S.h, S.n),
                                                    (T.h, T.n)])
                fuse_s = {}
                fuse_vv = {}
                for t in range(NT):
                    q = t % 2
                    src, guard = load(t)

                    def ftm(tn, t=t, q=q):
                        if t == 0:
                            tn.wait_ge(V.h, owb_tok)
                        if t >= 2:
                            tn.wait_ge(S.h, fuse_s[t - 2])
                        ins = None
                        for c in range(NCH):
                            for nn in range(2):
                                ins = tn.matmul(
                                    ps[q][:, nn * 512:(nn + 1) * 512],
                                    fmbuf[:, c * TOK + t * 128:
                                          c * TOK + t * 128 + 128],
                                    ow_b[:, c * 1024 + nn * 512:
                                         c * 1024 + nn * 512 + 512],
                                    start=(c == 0), stop=(c == NCH - 1))
                        T.inc(ins)
                    block.tensor(ftm)
                    ttok = T.n

                    def fsf(s, t=t, q=q, ttok=ttok):
                        s.wait_ge(T.h, ttok)
                        if t == 0:
                            s.wait_ge(V.h, rm_tok)
                            s.wait_ge(*p3_acc_tok)
                        if t >= 2:
                            s.wait_ge(V.h, fuse_vv[t - 2])
                        s.activation(silu_sc[q][:], acc[:, t * D:(t + 1) * D],
                                     ACT.Silu)
                        ins = s.activation(s_sc[q][:], ps[q][:], ACT.Copy,
                                           scale=rm_cols[:, t:t + 1])
                        S.inc(ins)
                    block.scalar(fsf)
                    fuse_s[t] = S.n
                    stok = S.n

                    def fvf(v, t=t, q=q, stok=stok, src=src, guard=guard):
                        v.wait_ge(S.h, stok)
                        v.wait_ge(*guard)
                        v.tensor_mul(s_sc[q][:], s_sc[q][:], silu_sc[q][:])
                        v.drain()
                        ins = v.tensor_add(x_new[:, t * D:(t + 1) * D],
                                           src, s_sc[q][:])
                        V.inc(ins)
                    block.vector(fvf)
                    fuse_vv[t] = V.n
                    mark(t, V.h, V.n)
                p4_xnew_tok = V.n
            dtap("x_new", [128, NT * D])
            dump("x_new", x_new[:], V.h, p4_xnew_tok)

        with ExitStack() as l1:
            with ExitStack() as ph:
                def src_of(t):
                    return x_new[:, t * D:(t + 1) * D], None
                stats_pass(ph, src_of, lambda *a: None,
                           first_wait=[(S.h, S.n), (T.h, T.n)])
                rr16 = sbuf("p5_rr", [128, NT], F32, ph)

                def fv(v):
                    v.tensor_mul(rr16[:], r_cols[:], r_cols[:])
                    v.drain()
                    ins = v.tensor_mul(rr16[:], rr16[:], v_cols[:])
                    V.inc(ins)
                block.vector(fv)
                vtok = V.n

                def fs(s, vtok=vtok):
                    s.wait_ge(V.h, vtok)
                    s.activation(rr16[:], rr16[:], ACT.Ln, bias=FFN1_EPS)
                    s.drain()
                    ins = s.activation(rr16[:], rr16[:], ACT.Exp, scale=-0.5)
                    S.inc(ins)
                block.scalar(fs)
                stok = S.n

                def fv2(v, stok=stok):
                    v.wait_ge(S.h, stok)
                    v.tensor_mul(rho_cols[:], rr16[:], r_cols[:])
                    ins = v.tensor_mul(nb2_cols[:], rr16[:], nbias_cols[:])
                    V.inc(ins)
                block.vector(fv2)
                p5_tok = V.n

            with ExitStack() as p6:
                ngb = sbuf("ngb", [128, D], F32, p6)
                nbb = sbuf("nbb", [128, D], F32, p6)
                sgb = sbuf("sgb", [128, D], F32, p6)
                sbb = sbuf("sbb", [128, D], F32, p6)
                slots_fm = sbuf("slots_fm", [128, NCH * 1024], BF16, p6)
                subX = p6.enter_context(ExitStack())

                with ExitStack() as ph:
                    rows32 = sbuf("p6_r32", [1, 4 * D], F32, ph)
                    rows_b = sbuf("p6_rb", [1, 4 * D], BF16, ph)
                    sc32 = sbuf("p6_sc32", [1, 1], F32, ph)
                    sccol = sbuf("p6_sccol", [128, 1], F32, ph)
                    sc_b = sbuf("p6_scb", [1, 1], BF16, ph)
                    pbc = psum("p6_pbc", [128, 512], F32, ph)
                    pscc = psum("p6_pscc", [128, 1], F32, ph)

                    def fsy(sy, tv=V.n, ts=S.n, te=T.n):
                        sy.wait_ge(V.h, tv)
                        sy.wait_ge(S.h, ts)
                        sy.wait_ge(T.h, te)
                        for i, src in enumerate((ng_in, nb_in, sg_in, sb_in)):
                            sy.dma_start(out=rows32[:, i * D:(i + 1) * D],
                                         in_=src).then_inc(DXS[0].h, 16)
                            DXS[0].n += 16
                        sy.dma_start(out=sc32[:], in_=sc1_in).then_inc(DXS[0].h, 16)
                        DXS[0].n += 16
                    block.sync(fsy)
                    rows_dma = DXS[0].n

                    def fvb(v):
                        v.wait_ge(DXS[0].h, rows_dma)
                        v.tensor_copy(rows_b[:], rows32[:])
                        ins = v.tensor_copy(sc_b[:], sc32[:])
                        V.inc(ins)
                    block.vector(fvb)
                    rows_b_tok = V.n

                    def ftb(tn):
                        tn.wait_ge(V.h, rows_b_tok)
                        ins = tn.matmul(pscc[:], ones_row[:], sc_b[:],
                                        start=True, stop=True)
                        T.inc(ins)
                    block.tensor(ftb)
                    scc_tok = T.n

                    def fvb2(v):
                        v.wait_ge(T.h, scc_tok)
                        ins = v.tensor_copy(sccol[:], pscc[:])
                        V.inc(ins)
                    block.vector(fvb2)
                    sccol_tok = V.n

                    prev_ev = sccol_tok
                    for i, dstb in enumerate((ngb, nbb, sgb, sbb)):
                        for nn in range(2):
                            def ftb2(tn, i=i, nn=nn, prev_ev=prev_ev):
                                tn.wait_ge(V.h, prev_ev)
                                ins = tn.matmul(
                                    pbc[:],
                                    ones_row[:],
                                    rows_b[:, i * D + nn * 512:
                                           i * D + nn * 512 + 512],
                                    start=True, stop=True)
                                T.inc(ins)
                            block.tensor(ftb2)
                            ttok = T.n

                            def fvb3(v, nn=nn, dstb=dstb, ttok=ttok):
                                v.wait_ge(T.h, ttok)
                                ins = v.tensor_copy(
                                    dstb[:, nn * 512:(nn + 1) * 512], pbc[:])
                                V.inc(ins)
                            block.vector(fvb3)
                            prev_ev = V.n

                    def fvb4(v):
                        v.drain()
                        for nn in range(2):
                            v.tensor_scalar(sgb[:, nn * 512:(nn + 1) * 512],
                                            sgb[:, nn * 512:(nn + 1) * 512],
                                            sccol[:], None, ALU.mult)
                            ins = v.tensor_scalar(
                                sbb[:, nn * 512:(nn + 1) * 512],
                                sbb[:, nn * 512:(nn + 1) * 512],
                                sccol[:], None, ALU.mult)
                        V.inc(ins)
                    block.vector(fvb4)
                    bcast_tok = V.n

                with ExitStack() as ph:
                    def src_of(t):
                        return x_new[:, t * D:(t + 1) * D], None
                    xn2_tok = fm_build(ph, src_of, lambda *a: None,
                                       rho_cols, nb2_cols, fmbuf,
                                       gb=(ngb, nbb),
                                       first_wait=[(V.h, bcast_tok)])
                dtap("xn2_fm", [128, NCH * TOK], BF16)
                dump("xn2_fm", fmbuf[:], V.h, xn2_tok)

                se_fm = sbuf("se_fm", [128, NCH * 1024], BF16, subX)
                with ExitStack() as ph:
                    set_ = [sbuf(f"p6_se{i}", [128, D], F32, ph) for i in range(2)]
                    sec = sbuf("p6_sec", [128, 8], F32, ph)
                    sex2 = sbuf("p6_sex2", [128, D], F32, ph)
                    setm = [sbuf(f"p6_setm{i}", [128, D], BF16, ph)
                            for i in range(2)]
                    ptr6 = [psum(f"p6_ptr{i}", [128, 1024], BF16, ph)
                            for i in range(2)]
                    se_ev = {}
                    se_mark = {}
                    for ch in range(NCH):
                        q = ch % 2

                        def fsy(sy, ch=ch, q=q, tv=V.n, ts=S.n, te=T.n):
                            if ch >= 2:
                                sy.wait_ge(*se_mark[ch - 2])
                            else:
                                sy.wait_ge(V.h, tv)
                                sy.wait_ge(S.h, ts)
                                sy.wait_ge(T.h, te)
                            sy.dma_start(out=set_[q][:],
                                         in_=se_in[ch * 128:(ch + 1) * 128, :]
                                         ).then_inc(DXS[q].h, 16)
                            DXS[q].n += 16
                        block.sync(fsy)
                        dguard = (DXS[q].h, DXS[q].n)

                        def fv(v, ch=ch, q=q, dguard=dguard):
                            v.wait_ge(*dguard)
                            if ch == 0:
                                v.wait_ge(V.h, bcast_tok)
                            v.tensor_mul(sex2[:], set_[q][:], set_[q][:])
                            v.drain()
                            v.reduce_sum(sec[:, 0:1], set_[q][:], axis=AX)
                            v.reduce_sum(sec[:, 1:2], sex2[:], axis=AX)
                            v.drain()
                            v.tensor_scalar(sec[:, 2:3], sec[:, 0:1], 1.0 / D,
                                            None, ALU.mult)
                            v.tensor_scalar(sec[:, 4:5], sec[:, 1:2], 1.0 / D,
                                            None, ALU.mult)
                            v.drain()
                            v.tensor_mul(sec[:, 3:4], sec[:, 2:3], sec[:, 2:3])
                            v.drain()
                            ins = v.tensor_sub(sec[:, 4:5], sec[:, 4:5],
                                               sec[:, 3:4])
                            V.inc(ins)
                        block.vector(fv)
                        vtok = V.n

                        def fs(s, q=q, vtok=vtok):
                            s.wait_ge(V.h, vtok)
                            s.activation(sec[:, 5:6], sec[:, 4:5], ACT.Ln,
                                         bias=FFN1_EPS)
                            s.drain()
                            ins = s.activation(sec[:, 6:7], sec[:, 5:6],
                                               ACT.Exp, scale=-0.5)
                            S.inc(ins)
                        block.scalar(fs)
                        stok = S.n

                        def fv2(v, q=q, stok=stok):
                            v.wait_ge(S.h, stok)
                            ins = v.scalar_tensor_tensor(
                                sec[:, 7:8], sec[:, 2:3], -1.0,
                                sec[:, 6:7], ALU.mult, ALU.mult)
                            V.inc(ins)
                        block.vector(fv2)
                        vtok2 = V.n

                        def fv3(v, q=q, vtok2=vtok2):
                            v.wait_ge(V.h, vtok2)
                            v.drain()
                            v.tensor_scalar(set_[q][:], set_[q][:],
                                            sec[:, 6:7], sec[:, 7:8],
                                            ALU.mult, ALU.add)
                            v.drain()
                            v.tensor_mul(set_[q][:], set_[q][:], sgb[:])
                            v.drain()
                            ins = v.tensor_add(setm[q][:], set_[q][:], sbb[:])
                            V.inc(ins)
                        block.vector(fv3)
                        se_mark[ch] = (V.h, V.n)
                        vtok3 = V.n

                        def ft(tn, ch=ch, q=q, vtok3=vtok3):
                            tn.wait_ge(V.h, vtok3)
                            if ch >= 2:
                                tn.wait_ge(V.h, se_ev[ch - 2])
                            for c in range(NCH):
                                ins = tn.transpose(
                                    ptr6[q][:, c * 128:(c + 1) * 128],
                                    setm[q][:, c * 128:(c + 1) * 128],
                                    ident_b[:])
                            T.inc(ins)
                        block.tensor(ft)
                        ttok = T.n

                        def fv4(v, ch=ch, q=q, ttok=ttok):
                            v.wait_ge(T.h, ttok)
                            dst = se_fm[:].rearrange(
                                "p (c n) -> p c n", c=NCH)[
                                :, :, ch * 128:(ch + 1) * 128]
                            ins = v.tensor_copy(
                                dst, ptr6[q][:].rearrange("p (c i) -> p c i", c=8))
                            V.inc(ins)
                        block.vector(fv4)
                        se_ev[ch] = V.n
                    se_tok = V.n
                dtap("se_fm", [128, NCH * 1024], BF16)
                dump("se_fm", se_fm[:], V.h, se_tok)

                dispatch_tm = sbuf("dispatch_tm", [128, NT * 1024], BF16, subX)
                with ExitStack() as ph:
                    lg = sbuf("p6_lg", [128, TOK], F32, ph)
                    dsp_b = sbuf("p6_dspb", [128, TOK], BF16, ph)
                    nmcol = sbuf("p6_nm", [128, 1], F32, ph)
                    smcol = sbuf("p6_sm", [128, 1], F32, ph)
                    rscol = sbuf("p6_rs", [128, 1], F32, ph)
                    pl = [psum(f"p6_pl{i}", [128, 512], F32, ph) for i in range(2)]
                    ptr6b = [psum(f"p6_ptb{i}", [128, 1024], BF16, ph)
                             for i in range(2)]

                    prev_tr = None
                    prev_exp = None
                    for ch in range(NCH):
                        mmtok = {}
                        for n in range(4):
                            def ft(tn, ch=ch, n=n):
                                if ch == 0 and n == 0:
                                    tn.wait_ge(V.h, xn2_tok)
                                    tn.wait_ge(V.h, se_tok)
                                if n >= 2:
                                    tn.wait_ge(V.h, mmtok[n - 2][1])
                                elif ch >= 1:
                                    tn.wait_ge(V.h, mmtok_prev[n + 2][1])
                                ins = None
                                for k in range(NCH):
                                    ins = tn.matmul(
                                        pl[n % 2][:],
                                        se_fm[:, k * 1024 + ch * 128:
                                              k * 1024 + ch * 128 + 128],
                                        fmbuf[:, k * TOK + n * 512:
                                              k * TOK + n * 512 + 512],
                                        start=(k == 0), stop=(k == NCH - 1))
                                T.inc(ins)
                            block.tensor(ft)
                            ttok = T.n

                            def fv(v, n=n, ttok=ttok, prev_exp=prev_exp):
                                v.wait_ge(T.h, ttok)
                                if n == 0 and prev_exp is not None:
                                    v.wait_ge(S.h, prev_exp)
                                ins = v.tensor_copy(
                                    lg[:, n * 512:(n + 1) * 512], pl[n % 2][:])
                                V.inc(ins)
                            block.vector(fv)
                            mmtok[n] = (T.n, V.n)
                        mmtok_prev = mmtok

                        def fv5(v):
                            v.drain()
                            v.reduce_max(nmcol[:], lg[:], axis=AX, negate=True)
                            ins = v.drain()
                            V.inc(ins)
                        block.vector(fv5)
                        nm_tok = V.n

                        def fse(s, nm_tok=nm_tok, prev_tr=prev_tr):
                            s.wait_ge(V.h, nm_tok)
                            if prev_tr is not None:
                                s.wait_ge(T.h, prev_tr)
                            ins = s.activation(dsp_b[:], lg[:], ACT.Exp,
                                               bias=nmcol[:],
                                               accum_out=smcol[:])
                            S.inc(ins)
                        block.scalar(fse)
                        exp_tok = S.n
                        prev_exp = exp_tok

                        def fv6(v, exp_tok=exp_tok):
                            v.wait_ge(S.h, exp_tok)
                            v.reciprocal(rscol[:], smcol[:])
                            v.drain()
                            ins = v.tensor_scalar(dsp_b[:], dsp_b[:], rscol[:],
                                                  None, ALU.mult)
                            V.inc(ins)
                        block.vector(fv6)
                        dsp_tok = V.n

                        ptok = {}
                        for grp in range(2):
                            def ft2(tn, ch=ch, grp=grp, dsp_tok=dsp_tok):
                                tn.wait_ge(V.h, dsp_tok)
                                if (ch, grp) != (0, 0):
                                    pass
                                for j in range(8):
                                    t = grp * 8 + j
                                    ins = tn.transpose(
                                        ptr6b[grp][:, j * 128:(j + 1) * 128],
                                        dsp_b[:, t * 128:(t + 1) * 128],
                                        ident_b[:])
                                T.inc(ins)
                            block.tensor(ft2)
                            ttok2 = T.n

                            def fv7(v, ch=ch, grp=grp, ttok2=ttok2):
                                v.wait_ge(T.h, ttok2)
                                dst = dispatch_tm[:].rearrange(
                                    "p (t n) -> p t n", n=1024)[
                                    :, grp * 8:(grp + 1) * 8,
                                    ch * 128:(ch + 1) * 128]
                                ins = v.tensor_copy(
                                    dst, ptr6b[grp][:].rearrange(
                                        "p (c i) -> p c i", c=8))
                                V.inc(ins)
                            block.vector(fv7)
                            ptok[grp] = V.n
                            prev_tr = ttok2
                    dispatch_tok = V.n

                    cstage = [sbuf(f"p6_cst{i}", [128, 1024], BF16, ph)
                              for i in range(2)]
                    cb_ev = {}
                    cb_dma = {}
                    for t in range(NT):
                        mmtok = {}
                        for nn in range(2):
                            def ft(tn, t=t, nn=nn):
                                if t >= 1:
                                    tn.wait_ge(V.h, cb_mm_prev[nn])
                                ins = None
                                for k in range(NCH):
                                    ins = tn.matmul(
                                        pl[nn][:],
                                        fmbuf[:, k * TOK + t * 128:
                                              k * TOK + t * 128 + 128],
                                        se_fm[:, k * 1024 + nn * 512:
                                              k * 1024 + nn * 512 + 512],
                                        start=(k == 0), stop=(k == NCH - 1))
                                T.inc(ins)
                            block.tensor(ft)
                            ttok = T.n

                            def fv(v, nn=nn, ttok=ttok, prev_exp=prev_exp):
                                v.wait_ge(T.h, ttok)
                                if nn == 0:
                                    v.wait_ge(S.h, prev_exp)
                                ins = v.tensor_copy(
                                    lg[:, nn * 512:(nn + 1) * 512], pl[nn][:])
                                V.inc(ins)
                            block.vector(fv)
                            mmtok[nn] = V.n
                        cb_mm_prev = mmtok

                        def fv5(v, t=t):
                            v.drain()
                            v.reduce_max(nmcol[:], lg[:, 0:1024], axis=AX,
                                         negate=True)
                            ins = v.drain()
                            V.inc(ins)
                        block.vector(fv5)
                        nm_tok = V.n

                        def fse(s, nm_tok=nm_tok, prev_tr=prev_tr):
                            s.wait_ge(V.h, nm_tok)
                            s.wait_ge(T.h, prev_tr)
                            ins = s.activation(dsp_b[:, 0:1024], lg[:, 0:1024],
                                               ACT.Exp, bias=nmcol[:],
                                               accum_out=smcol[:])
                            S.inc(ins)
                        block.scalar(fse)
                        exp_tok = S.n
                        prev_exp = exp_tok

                        def fv6(v, exp_tok=exp_tok):
                            v.wait_ge(S.h, exp_tok)
                            v.reciprocal(rscol[:], smcol[:])
                            v.drain()
                            ins = v.tensor_scalar(dsp_b[:, 0:1024],
                                                  dsp_b[:, 0:1024], rscol[:],
                                                  None, ALU.mult)
                            V.inc(ins)
                        block.vector(fv6)
                        cmb_tok = V.n

                        def ft2(tn, t=t, cmb_tok=cmb_tok):
                            tn.wait_ge(V.h, cmb_tok)
                            if t >= 1:
                                tn.wait_ge(V.h, cb_ev[t - 1])
                            for j in range(8):
                                ins = tn.transpose(
                                    ptr6b[0][:, j * 128:(j + 1) * 128],
                                    dsp_b[:, j * 128:(j + 1) * 128],
                                    ident_b[:])
                            T.inc(ins)
                        block.tensor(ft2)
                        ttok2 = T.n

                        def fv7(v, t=t, ttok2=ttok2):
                            v.wait_ge(T.h, ttok2)
                            if t >= 2:
                                v.wait_ge(*cb_dma[t - 2])
                            ins = v.tensor_copy(cstage[t % 2][:], ptr6b[0][:])
                            V.inc(ins)
                        block.vector(fv7)
                        cb_ev[t] = V.n
                        vtok7 = V.n
                        prev_tr = ttok2

                        def fsy7(sy, t=t, vtok7=vtok7):
                            sy.wait_ge(V.h, vtok7)
                            sy.dma_start(
                                out=comb_sp[:, t * 1024:(t + 1) * 1024],
                                in_=cstage[t % 2][:]).then_inc(
                                    DXS[t % 2].h, 16)
                            DXS[t % 2].n += 16
                        block.sync(fsy7)
                        cb_dma[t] = (DXS[t % 2].h, DXS[t % 2].n)
                    combine_tok = V.n
                    cb_dma_done = [cb_dma[NT - 2], cb_dma[NT - 1]]
                dtap("dispatch_tm", [128, NT * 1024], BF16)
                dump("dispatch_tm", dispatch_tm[:], V.h, dispatch_tok)
                if dtap("combine_sp", [128, NT * 1024], BF16) is not None:
                    def fdc(sy, dprev=DO.n):
                        sy.wait_ge(DO.h, dprev)
                        sy.wait_ge(*cb_dma_done[0])
                        sy.wait_ge(*cb_dma_done[1])
                        sy.dma_start(out=dbg_out["combine_sp"],
                                     in_=comb_sp).then_inc(DO.h, 16)
                        DO.n += 16
                    block.sync(fdc)

                with ExitStack() as ph:
                    x2a = [sbuf(f"p6_x2a{i}", [128, D], F32, ph) for i in range(2)]
                    x2tm = [sbuf(f"p6_x2tm{i}", [128, D], BF16, ph)
                            for i in range(2)]
                    psl = [psum(f"p6_psl{k}", [128, 512], F32, ph)
                           for k in range(NCH)]
                    afft = {}
                    for gi in range(2 * NT):
                        eh, t = divmod(gi, NT)
                        q = gi % 2

                        def fva(v, gi=gi, t=t, q=q):
                            if gi == 0:
                                v.wait_ge(V.h, p5_tok)
                            if gi >= 2:
                                v.wait_ge(T.h, afft[gi - 2])
                            v.tensor_scalar(x2a[q][:],
                                            x_new[:, t * D:(t + 1) * D],
                                            rho_cols[:, t:t + 1],
                                            nb2_cols[:, t:t + 1],
                                            ALU.mult, ALU.add)
                            v.drain()
                            v.tensor_mul(x2a[q][:], x2a[q][:], ngb[:])
                            v.drain()
                            ins = v.tensor_add(x2tm[q][:], x2a[q][:], nbb[:])
                            V.inc(ins)
                        block.vector(fva)
                        vtok = V.n

                        def fta(tn, gi=gi, t=t, q=q, eh=eh, vtok=vtok):
                            tn.wait_ge(V.h, vtok)
                            if gi == 0:
                                tn.wait_ge(V.h, combine_tok)
                            if eh == 1 and t == 0:
                                tn.wait_ge(V.h, slots_ev0)
                            ins = None
                            for k in range(NCH):
                                ins = tn.matmul(
                                    psl[k][:],
                                    x2tm[q][:, k * 128:(k + 1) * 128],
                                    dispatch_tm[:, t * 1024 + eh * 512:
                                                t * 1024 + eh * 512 + 512],
                                    start=(t == 0), stop=(t == NT - 1))
                            T.inc(ins)
                        block.tensor(fta)
                        afft[gi] = T.n

                        if t == NT - 1:
                            mme = T.n

                            def fve(v, eh=eh, mme=mme):
                                v.wait_ge(T.h, mme)
                                ins = None
                                for k in range(NCH):
                                    ins = v.tensor_copy(
                                        slots_fm[:, k * 1024 + eh * 512:
                                                 k * 1024 + eh * 512 + 512],
                                        psl[k][:])
                                V.inc(ins)
                            block.vector(fve)
                            if eh == 0:
                                slots_ev0 = V.n
                    slots_tok = V.n
                dtap("slots_fm", [128, NCH * 1024], BF16)
                dump("slots_fm", slots_fm[:], V.h, slots_tok)
                subX.close()
                eout_b = sbuf("eout_b", [128, NCH * D], BF16, p6)

                with ExitStack() as ph:
                    wch = [sbuf(f"p6_wch{i}", [128, 8192], BF16, ph)
                           for i in range(2)]
                    gate_sb = sbuf("p6_gate", [64, F1], BF16, ph)
                    h1 = sbuf("p6_h1", [64, F1], BF16, ph)
                    h1T = sbuf("p6_h1T", [128, 1024], BF16, ph)
                    pg1 = [psum(f"p6_pg1{i}", [64, 512], F32, ph)
                           for i in range(4)]
                    pe1 = [psum(f"p6_pe1{i}", [64, 512], F32, ph)
                           for i in range(2)]
                    pth = [psum(f"p6_pth{i}", [128, 256], BF16, ph)
                           for i in range(2)]
                    FW = Tok("sem_fw")
                    EW = [Tok("sem_ew0"), Tok("sem_ew1")]
                    BDG = Tok("sem_bdg")
                    BDU = Tok("sem_bdu")
                    BDD = Tok("sem_bdd")
                    bge = sbuf("p6_bge", [1, F1], BF16, ph)
                    bue = sbuf("p6_bue", [1, F1], BF16, ph)
                    bde = sbuf("p6_bde", [1, D], BF16, ph)
                    nload = 0

                    def emit_edma(e):
                        nonlocal nload
                        if e >= E1:
                            return
                        srcs = []
                        for g in range(2):
                            srcs.append(("g", wg1B[e, g * 512:(g + 1) * 512, :]))
                        for g in range(2):
                            srcs.append(("u", wu1B[e, g * 512:(g + 1) * 512, :]))
                        for g in range(4):
                            srcs.append(("d", wd1B[e, g * 512:(g + 1) * 512, :]))

                        def fgb(g_, e=e, te=T.n, tv=V.n, ts=S.n):
                            if e >= 1:
                                g_.wait_ge(FW.h, 8 * e)
                            else:
                                g_.wait_ge(T.h, te)
                                g_.wait_ge(V.h, tv)
                                g_.wait_ge(S.h, ts)
                                g_.wait_ge(*cb_dma_done[0])
                                g_.wait_ge(*cb_dma_done[1])
                            g_.dma_start(out=bge[:],
                                         in_=bg1_in[:, e * F1:(e + 1) * F1]
                                         ).then_inc(BDG.h, 16)
                            g_.dma_start(out=bue[:],
                                         in_=bu1_in[:, e * F1:(e + 1) * F1]
                                         ).then_inc(BDU.h, 16)
                            g_.dma_start(out=bde[:],
                                         in_=bd1_in[:, e * D:(e + 1) * D]
                                         ).then_inc(BDD.h, 16)
                            BDG.n += 16
                            BDU.n += 16
                            BDD.n += 16
                        block.gpsimd(fgb)

                        def fg2(sy, e=e, srcs=srcs, te=T.n, tv=V.n):
                            nonlocal nload
                            sy.wait_ge(PC.h, pc[f"d1{e}"])
                            if e >= 1:
                                sy.wait_ge(FW.h, 8 * e)
                            else:
                                sy.wait_ge(T.h, te)
                                sy.wait_ge(V.h, tv)
                                sy.wait_ge(*cb_dma_done[0])
                                sy.wait_ge(*cb_dma_done[1])
                            WSEM = EW
                            for kind, src in srcs:
                                slot = nload % 2
                                if nload >= 2:
                                    pl = nload - 2
                                    r = pl % 8
                                    base = 8 * (pl // 8)
                                    req = base + (2 if r < 2 else
                                                  4 if r < 4 else
                                                  6 if r < 6 else 8)
                                    sy.wait_ge(FW.h, req)
                                w = src.shape[1]
                                dst = wch[slot][:, 0:4 * w].rearrange(
                                    "p (c i) -> p c i", c=4)
                                sy.dma_start(
                                    out=dst,
                                    in_=src.rearrange("(c p) i -> p c i", p=128)
                                ).then_inc(WSEM[slot].h, 16)
                                WSEM[slot].n += 16
                                nload += 1
                        block.sync(fg2)
                    emit_edma(0)

                    eout_ev = {}
                    nuse = 0
                    prev_h = None
                    for e in range(E1):
                        ce, re = e // 2, 64 * (e % 2)
                        def ftg(tn, e=e, nuse=nuse, prev_h=prev_h,
                                t_pre=T.n):
                            if e == 0:
                                tn.wait_ge(V.h, slots_tok)
                            else:
                                tn.wait_ge(V.h, prev_h)
                            tn.wait_ge(BDG.h, (e + 1) * 16)
                            for fs_ in range(4):
                                tn.matmul(pg1[fs_][:], ones_row[0:1, 0:64],
                                          bge[0:1, fs_ * 512:
                                              fs_ * 512 + 512],
                                          start=True, stop=False)
                            for k in range(NCH):
                                li = nuse + k // 4
                                gslot = li % 2
                                tn.wait_ge(EW[gslot].h, (li // 2 + 1) * 16)
                                ins = None
                                for fs_ in range(4):
                                    ins = tn.matmul(
                                        pg1[fs_][:],
                                        slots_fm[:, k * 1024 + e * 64:
                                                 k * 1024 + e * 64 + 64],
                                        wch[gslot][:, (k % 4) * F1 + fs_ * 512:
                                                   (k % 4) * F1 + fs_ * 512 + 512],
                                        start=False, stop=(k == NCH - 1))
                                if k % 4 == 3:
                                    T.inc(ins)
                            tn.wait_ge(T.h, t_pre + 2)
                            tn.nop().then_inc(FW.h, 2)
                        block.tensor(ftg)
                        FW.n += 2
                        gmm = T.n

                        def fsg(s, gmm=gmm, prev_h=prev_h):
                            s.wait_ge(T.h, gmm)
                            if prev_h is not None:
                                s.wait_ge(V.h, prev_h)
                            ins = None
                            for fs_ in range(4):
                                ins = s.activation(
                                    gate_sb[:, fs_ * 512:(fs_ + 1) * 512],
                                    pg1[fs_][:], ACT.Gelu)
                            S.inc(ins)
                        block.scalar(fsg)
                        gelu_tok = S.n

                        def ftu(tn, e=e, nuse=nuse, gelu_tok=gelu_tok,
                                t_pre=T.n):
                            tn.wait_ge(S.h, gelu_tok)
                            tn.wait_ge(BDU.h, (e + 1) * 16)
                            for fs_ in range(4):
                                tn.matmul(pg1[fs_][:], ones_row[0:1, 0:64],
                                          bue[0:1, fs_ * 512:
                                              fs_ * 512 + 512],
                                          start=True, stop=False)
                            for k in range(NCH):
                                li = nuse + 2 + k // 4
                                uslot = li % 2
                                tn.wait_ge(EW[uslot].h, (li // 2 + 1) * 16)
                                ins = None
                                for fs_ in range(4):
                                    ins = tn.matmul(
                                        pg1[fs_][:],
                                        slots_fm[:, k * 1024 + e * 64:
                                                 k * 1024 + e * 64 + 64],
                                        wch[uslot][:, (k % 4) * F1 + fs_ * 512:
                                                   (k % 4) * F1 + fs_ * 512 + 512],
                                        start=False, stop=(k == NCH - 1))
                                if k % 4 == 3:
                                    T.inc(ins)
                            tn.wait_ge(T.h, t_pre + 2)
                            tn.nop().then_inc(FW.h, 2)
                        block.tensor(ftu)
                        FW.n += 2
                        umm = T.n

                        def fvh(v, umm=umm):
                            v.wait_ge(T.h, umm)
                            ins = None
                            for fs_ in range(4):
                                ins = v.tensor_mul(
                                    h1[:, fs_ * 512:(fs_ + 1) * 512],
                                    gate_sb[:, fs_ * 512:(fs_ + 1) * 512],
                                    pg1[fs_][:])
                            V.inc(ins)
                        block.vector(fvh)
                        h_tok = V.n

                        htok = {}
                        for grp in range(4):
                            def fth(tn, grp=grp, h_tok=h_tok):
                                tn.wait_ge(V.h, h_tok)
                                if grp >= 2:
                                    tn.wait_ge(V.h, htok[grp - 2])
                                for j in range(4):
                                    piece = grp * 4 + j
                                    ins = tn.transpose(
                                        pth[grp % 2][:, j * 64:(j + 1) * 64],
                                        h1[0:64, piece * 128:(piece + 1) * 128],
                                        ident_b[0:64, 0:64])
                                T.inc(ins)
                            block.tensor(fth)
                            tht = T.n

                            def fvt(v, grp=grp, tht=tht):
                                v.wait_ge(T.h, tht)
                                ins = v.tensor_copy(
                                    h1T[:, grp * 256:(grp + 1) * 256],
                                    pth[grp % 2][:])
                                V.inc(ins)
                            block.vector(fvt)
                            htok[grp] = V.n
                        h1T_tok = V.n

                        def ftd1(tn, e=e, nuse=nuse, h1T_tok=h1T_tok,
                                 t_pre=T.n):
                            tn.wait_ge(V.h, h1T_tok)
                            if e >= 1:
                                tn.wait_ge(V.h, eout_ev[e - 1])
                            tn.wait_ge(BDD.h, (e + 1) * 16)
                            for nn in range(2):
                                tn.matmul(pe1[nn][:], ones_row[0:1, 0:64],
                                          bde[0:1, nn * 512:
                                              nn * 512 + 512],
                                          start=True, stop=False)
                            for k16 in range(16):
                                if k16 == 8:
                                    tn.wait_ge(T.h, t_pre + 2)
                                    tn.nop().then_inc(FW.h, 2)
                                li = nuse + 4 + k16 // 4
                                dslot = li % 2
                                tn.wait_ge(EW[dslot].h, (li // 2 + 1) * 16)
                                ins = None
                                for nn in range(2):
                                    ins = tn.matmul(
                                        pe1[nn][:],
                                        h1T[:, k16 * 64:(k16 + 1) * 64],
                                        wch[dslot][:, (k16 % 4) * 1024 + nn * 512:
                                                   (k16 % 4) * 1024 + nn * 512 + 512],
                                        start=False, stop=(k16 == 15))
                                if k16 % 4 == 3:
                                    T.inc(ins)
                            tn.wait_ge(T.h, t_pre + 4)
                            tn.nop().then_inc(FW.h, 2)
                        block.tensor(ftd1)
                        FW.n += 4
                        dmm = T.n

                        def fvev(v, e=e, ce=ce, re=re, dmm=dmm):
                            v.wait_ge(T.h, dmm)
                            ins = None
                            for nn in range(2):
                                ins = v.tensor_copy(
                                    eout_b[re:re + 64,
                                           ce * 1024 + nn * 512:
                                           ce * 1024 + nn * 512 + 512],
                                    pe1[nn][:])
                            V.inc(ins)
                        block.vector(fvev)
                        eout_ev[e] = V.n
                        prev_h = h_tok
                        nuse += 8
                        emit_edma(e + 1)
                    eout_tok = V.n
                dtap("eout", [128, NCH * D], BF16)
                dump("eout", eout_b[:], V.h, eout_tok)

                with ExitStack() as ph:
                    CR = [Tok("sem_cr0"), Tok("sem_cr1")]
                    GS = [Tok("sem_gs0"), Tok("sem_gs1")]
                    g1t = [sbuf(f"p6_g1{i}", [128, D], BF16, ph) for i in range(2)]
                    cst2 = [sbuf(f"p6_cs2{i}", [128, 1024], BF16, ph)
                            for i in range(2)]
                    pcm = [psum(f"p6_pcm{i}", [128, 1024], F32, ph)
                           for i in range(2)]
                    g1_dma = {}
                    cmb_tok2 = {}
                    for t in range(NT):
                        q = t % 2

                        def fsyc(sy, t=t, q=q, te=T.n, tv=V.n, ts=S.n):
                            if t < 2:
                                sy.wait_ge(*cb_dma_done[0])
                                sy.wait_ge(*cb_dma_done[1])
                                sy.wait_ge(T.h, te)
                                sy.wait_ge(V.h, tv)
                                sy.wait_ge(S.h, ts)
                            else:
                                sem, val = cmb_tok2[t - 2]
                                sy.wait_ge(T.h, val)
                            sy.dma_start(out=cst2[q][:],
                                         in_=comb_sp[:, t * 1024:(t + 1) * 1024]
                                         ).then_inc(CR[q].h, 16)
                            CR[q].n += 16
                        block.sync(fsyc)
                        crel = (CR[q].h, CR[q].n)

                        def ftg1(tn, t=t, q=q, crel=crel):
                            tn.wait_ge(*crel)
                            if t == 0:
                                tn.wait_ge(V.h, eout_tok)
                            if t >= 2:
                                sem, val = g1_dma[t - 2]
                                tn.wait_ge(sem, val)
                            ins = None
                            for k in range(NCH):
                                for nn in range(2):
                                    ins = tn.matmul(
                                        pcm[q][:, nn * 512:(nn + 1) * 512],
                                        cst2[q][:, k * 128:(k + 1) * 128],
                                        eout_b[:, k * 1024 + nn * 512:
                                               k * 1024 + nn * 512 + 512],
                                        start=(k == 0), stop=(k == NCH - 1))
                            T.inc(ins)
                        block.tensor(ftg1)
                        ttok = T.n
                        cmb_tok2[t] = (T.h, ttok)

                        def fsg1(s, t=t, q=q, ttok=ttok):
                            s.wait_ge(T.h, ttok)
                            if t >= 2:
                                s.wait_ge(*g1_dma[t - 2])
                            ins = s.activation(g1t[q][:], pcm[q][:], ACT.Silu)
                            S.inc(ins)
                        block.scalar(fsg1)
                        stok = S.n

                        def fsyg(sy, t=t, q=q, stok=stok):
                            sy.wait_ge(S.h, stok)
                            sy.dma_start(out=g1_spill[t * 128:(t + 1) * 128, :],
                                         in_=g1t[q][:]).then_inc(GS[q].h, 16)
                            GS[q].n += 16
                        block.sync(fsyg)
                        g1_dma[t] = (GS[q].h, GS[q].n)
                    g1_done = [g1_dma[NT - 2], g1_dma[NT - 1]]

            attnout = sbuf("attnout", [128, NCH * TOK], BF16, l1)
            causal_b = sbuf("causal_b", [128, 4 * 512], BF16, l1)

            CAU = Tok("sem_cau")

            def fgc(g, te=T.n, tv=V.n, ts=S.n):
                g.wait_ge(T.h, te)
                g.wait_ge(V.h, tv)
                g.wait_ge(S.h, ts)
                g.dma_start(out=causal_b[:], in_=causal_in).then_inc(CAU.h, 16)
                CAU.n += 16
            block.gpsimd(fgc)

            with ExitStack() as ph:
                def src_of(t):
                    return x_new[:, t * D:(t + 1) * D], None
                xn1_tok = fm_build(ph, src_of, lambda *a: None,
                                   r_cols, nbias_cols, fmbuf,
                                   first_wait=[g1_done[0], g1_done[1],
                                               (S.h, S.n)])
            dtap("xn1_fm", [128, NCH * TOK], BF16)
            dump("xn1_fm", fmbuf[:], V.h, xn1_tok)

            with ExitStack() as p7:
                Qf = sbuf("Qf", [128, 2 * TOK], BF16, p7)
                Kf = sbuf("Kf", [128, 2 * TOK], BF16, p7)
                Vp = sbuf("Vp", [128, NT * 260], BF16, p7)

                wsl_last = {}
                srd_last = None
                core_s_last = None
                ao_last = None
                wsl = [sbuf(f"p7_w{i}", [128, NCH * 256], BF16, p7)
                       for i in range(2)]
                brow2 = [sbuf(f"p7_brb{i}", [1, 256], BF16, p7)
                         for i in range(2)]
                qsq = sbuf("p7_qsq", [128, 1024], F32, p7)
                ssq_all = sbuf("p7_ssqa", [128, 64], F32, p7)
                rn_all = sbuf("p7_rna", [128, 64], F32, p7)
                qhat4 = [sbuf(f"p7_qhat{i}", [128, 256], BF16, p7)
                         for i in range(4)]
                exp_sb = sbuf("p7_exps", [128, 16 * 512], BF16, p7)
                dn4 = sbuf("p7_dn", [1, 2048], F32, p7)
                rd4 = sbuf("p7_rd", [1, 2048], BF16, p7)
                prd_sb4 = sbuf("p7_prds", [64, 2048], BF16, p7)
                qraw = exp_sb
                for hq in range(4):
                    pq2 = p7.enter_context(ExitStack())
                    pqk = [psum(f"p7_pqk{i}", [128, 256], F32, pq2)
                           for i in range(4)]
                    ptrbuf = psum("p7_ptr", [128, 1024], BF16, pq2)
                    ptr7 = [ptrbuf[:, i * 256:(i + 1) * 256] for i in range(4)]

                    qkv_ev = {}
                    for which, (b_in, isv) in enumerate(
                            ((bq_in, False), (bk_in, False), (bv_in, True))):
                        wq_ = which % 2
                        wB = (wqB, wkB, wvB)[which]
                        WQL = Tok(f"sem_wql{hq}_{which}")
                        BRL = Tok(f"sem_brl{hq}_{which}")

                        def fgw(sy, hq=hq, wq_=wq_, wB=wB, WQL=WQL,
                                which=which, te=T.n, tv=V.n, ts=S.n,
                                wfree=wsl_last.get(which % 2)):
                            sy.wait_ge(PC.h, pc[("wq", "wk", "wv")[which]])
                            if wfree is not None:
                                sy.wait_ge(T.h, wfree)
                            else:
                                sy.wait_ge(T.h, te)
                                sy.wait_ge(V.h, tv)
                                sy.wait_ge(S.h, ts)
                            sy.dma_start(
                                out=wsl[wq_][:].rearrange("p (c i) -> p c i",
                                                          c=NCH),
                                in_=wB[:, hq * 256:(hq + 1) * 256]
                                .rearrange("(c p) i -> p c i", p=128)
                            ).then_inc(WQL.h, 16)
                            WQL.n += 16
                        block.sync(fgw)

                        def fgwb(g, hq=hq, b_in=b_in, BRL=BRL, wq_=wq_,
                                 te=T.n, wfree=wsl_last.get(which % 2)):
                            if wfree is not None:
                                g.wait_ge(T.h, wfree)
                            else:
                                g.wait_ge(T.h, te)
                            g.dma_start(out=brow2[wq_][:],
                                        in_=b_in[:, hq * 256:(hq + 1) * 256]
                                        ).then_inc(BRL.h, 16)
                            BRL.n += 16
                        block.gpsimd(fgwb)

                        sa_tok = {}
                        vraw_tok = {}
                        for t in range(NT):
                            q = t % 4

                            def ftqkv(tn, t=t, q=q, wq_=wq_, which=which,
                                      hq=hq, WQL=WQL, BRL=BRL, isv=isv,
                                      ao_prev=ao_last, s_prev=core_s_last):
                                if t == 0:
                                    tn.wait_ge(WQL.h, 16)
                                    tn.wait_ge(BRL.h, 16)
                                    if hq == 0 and which == 0:
                                        tn.wait_ge(V.h, xn1_tok)
                                    if which == 0 and ao_prev is not None:
                                        tn.wait_ge(V.h, ao_prev)
                                        tn.wait_ge(S.h, s_prev)
                                if t >= 4:
                                    tn.wait_ge(V.h, vraw_tok[t - 4])
                                    if not isv:
                                        tn.wait_ge(S.h, sa_tok[t - 4])
                                tn.matmul(pqk[q][:], ones_row[:],
                                          brow2[wq_][:],
                                          start=True, stop=False)
                                ins = None
                                for k in range(NCH):
                                    ins = tn.matmul(
                                        pqk[q][:],
                                        fmbuf[:, k * TOK + t * 128:
                                              k * TOK + t * 128 + 128],
                                        wsl[wq_][:, k * 256:(k + 1) * 256],
                                        start=False, stop=(k == NCH - 1))
                                T.inc(ins)
                            block.tensor(ftqkv)
                            ttok = T.n

                            if not isv:
                                def fsA(sc, q=q, t=t, ttok=ttok):
                                    sc.wait_ge(T.h, ttok)
                                    ins = None
                                    for hh in range(4):
                                        ins = sc.activation(
                                            qsq[:, q * 256 + hh * 64:
                                                q * 256 + hh * 64 + 64],
                                            pqk[q][:, hh * 64:(hh + 1) * 64],
                                            ACT.Square,
                                            accum_out=ssq_all[:, t * 4 + hh:
                                                              t * 4 + hh + 1])
                                    S.inc(ins)
                                block.scalar(fsA)
                                sa_tok[t] = S.n

                                def fvraw(v, q=q, t=t, ttok=ttok):
                                    v.wait_ge(T.h, ttok)
                                    ins = v.tensor_copy(
                                        qraw[:, t * 256:(t + 1) * 256],
                                        pqk[q][:])
                                    V.inc(ins)
                                block.vector(fvraw)
                                vraw_tok[t] = V.n
                            else:
                                def fvv(v, t=t, q=q, ttok=ttok):
                                    v.wait_ge(T.h, ttok)
                                    dst = Vp[:, t * 260:(t + 1) * 260
                                             ].rearrange(
                                        "p (h x) -> p h x", h=4)[:, :, 0:64]
                                    v.tensor_copy(
                                        dst, pqk[q][:].rearrange(
                                            "p (c i) -> p c i", c=4))
                                    ones_dst = Vp[:, t * 260:(t + 1) * 260
                                                  ].rearrange(
                                        "p (h x) -> p h x", h=4)[:, :, 64:65]
                                    v.memset(ones_dst, 1.0)
                                    ins = v.drain()
                                    V.inc(ins)
                                block.vector(fvv)
                                vraw_tok[t] = V.n
                                qkv_ev[t] = V.n

                        if isv:
                            wsl_last[which % 2] = T.n
                            continue

                        bias_const = QK_BIAS if which == 0 else 0.0

                        def fvmx(v, sa15=sa_tok[NT - 1]):
                            v.wait_ge(S.h, sa15)
                            ins = v.tensor_scalar_max(
                                ssq_all[:], ssq_all[:], 1e-16)
                            V.inc(ins)
                        block.vector(fvmx)
                        mx_tok = V.n

                        def fsq(sc, mx_tok=mx_tok, bias_const=bias_const):
                            sc.wait_ge(V.h, mx_tok)
                            sc.activation(rn_all[:], ssq_all[:], ACT.Ln)
                            sc.drain()
                            ins = sc.activation(rn_all[:], rn_all[:],
                                                ACT.Exp, scale=-0.5,
                                                bias=bias_const)
                            S.inc(ins)
                        block.scalar(fsq)
                        rn_tok = S.n

                        qhat_rd = {}
                        dstbuf = Qf if which == 0 else Kf
                        for t in range(NT):
                            q = t % 4

                            def fvq2(v, q=q, t=t, rn_tok=rn_tok,
                                     qh_prev=qhat_rd.get(t - 4)):
                                if t == 0:
                                    v.wait_ge(S.h, rn_tok)
                                if qh_prev is not None:
                                    v.wait_ge(T.h, qh_prev)
                                ins = None
                                for hh in range(4):
                                    ins = v.tensor_scalar(
                                        qhat4[q][:, hh * 64:(hh + 1) * 64],
                                        qraw[:, t * 256 + hh * 64:
                                             t * 256 + hh * 64 + 64],
                                        rn_all[:, t * 4 + hh:
                                               t * 4 + hh + 1], None,
                                        ALU.mult)
                                V.inc(ins)
                            block.vector(fvq2)
                            vt2 = V.n

                            def ftt(tn, q=q, vt2=vt2, t=t):
                                tn.wait_ge(V.h, vt2)
                                if t >= 4:
                                    tn.wait_ge(V.h, qkv_ev[t - 4])
                                for j in range(2):
                                    ins = tn.transpose(
                                        ptrbuf[:, q * 256 + j * 128:
                                               q * 256 + j * 128 + 128],
                                        qhat4[q][:, j * 128:(j + 1) * 128],
                                        ident_b[:])
                                T.inc(ins)
                            block.tensor(ftt)
                            tt2 = T.n
                            qhat_rd[t] = tt2

                            def fvq3(v, t=t, q=q, tt2=tt2, dstbuf=dstbuf):
                                v.wait_ge(T.h, tt2)
                                dst = dstbuf[:].rearrange(
                                    "p (c n) -> p c n", c=2)[
                                    :, :, t * 128:(t + 1) * 128]
                                ins = v.tensor_copy(
                                    dst, ptr7[q].rearrange(
                                        "p (c i) -> p c i", c=2))
                                V.inc(ins)
                            block.vector(fvq3)
                            qkv_ev[t] = V.n
                        wsl_last[which % 2] = T.n
                    qkv_done = V.n
                    pq2.close()

                    with ExitStack() as pa:
                        psc = [psum(f"p7_psc{i}", [128, 512], F32, pa)
                               for i in range(2)]
                        pav = [psum(f"p7_pav{i}", [128, 512], F32, pa)
                               for i in range(4)]
                        prd = psum("p7_prd", [64, 512], F32, pa)
                        ao_ev = {}
                        gidx = 0
                        for h8 in range(4):
                            cc = 2 * hq + h8 // 2
                            row0 = 64 * (h8 % 2)
                            ccl = h8 // 2
                            av_tok = {}
                            for js in range(4):
                                nk = 4 * js + 4
                                npair = (nk + 1) // 2
                                t0 = T.n
                                s0 = S.n

                                def fA(tn, h8=h8, js=js, nk=nk, ccl=ccl,
                                       row0=row0, t0=t0, s0=s0, gidx=gidx,
                                       qkv_done=qkv_done, hq=hq):
                                    if gidx == 0:
                                        tn.wait_ge(V.h, qkv_done)
                                    for kt in range(nk):
                                        pp = kt % 2
                                        if kt >= 2:
                                            tn.wait_ge(S.h, s0 + kt - 1)
                                        elif gidx > 0 or hq > 0:
                                            tn.wait_ge(S.h, s0)
                                        ins = tn.matmul(
                                            psc[pp][:],
                                            Kf[row0:row0 + 64,
                                               ccl * TOK + kt * 128:
                                               ccl * TOK + kt * 128 + 128],
                                            Qf[row0:row0 + 64,
                                               ccl * TOK + js * 512:
                                               ccl * TOK + js * 512 + 512],
                                            start=True, stop=True)
                                        T.inc(ins)
                                block.tensor(fA)

                                def fS(sc, nk=nk, t0=t0):
                                    for kt in range(nk):
                                        sc.wait_ge(T.h, t0 + kt + 1)
                                        ins = sc.activation(
                                            exp_sb[:, kt * 512:
                                                   (kt + 1) * 512],
                                            psc[kt % 2][:], ACT.Exp)
                                        S.inc(ins)
                                block.scalar(fS)

                                mask_tok = {}

                                def fM(v, js=js, nk=nk, s0=s0, hq=hq, h8=h8):
                                    for kt in range(4 * js, nk):
                                        v.wait_ge(S.h, s0 + kt + 1)
                                        if hq == 0 and h8 == 0 and js == 0 \
                                                and kt == 0:
                                            v.wait_ge(CAU.h, 16)
                                        rr = kt * 128 - js * 512
                                        ins = v.tensor_mul(
                                            exp_sb[:, kt * 512:(kt + 1) * 512],
                                            exp_sb[:, kt * 512:(kt + 1) * 512],
                                            causal_b[:, (rr // 128) * 512:
                                                     (rr // 128) * 512 + 512])
                                        V.inc(ins)
                                        mask_tok[kt] = V.n
                                block.vector(fM)

                                def fB(tn, h8=h8, js=js, nk=nk, s0=s0,
                                       gidx=gidx, mask_tok=mask_tok,
                                       prev=ao_ev.get(gidx - 4)):
                                    for kt in range(nk):
                                        tn.wait_ge(S.h, s0 + kt + 1)
                                        if kt >= 4 * js:
                                            tn.wait_ge(V.h, mask_tok[kt])
                                        if kt == 0 and prev is not None:
                                            tn.wait_ge(V.h, prev)
                                        ins = tn.matmul(
                                            pav[js][0:65, :],
                                            Vp[:, kt * 260 + h8 * 65:
                                               kt * 260 + h8 * 65 + 65],
                                            exp_sb[:, kt * 512:(kt + 1) * 512],
                                            start=(kt == 0),
                                            stop=(kt == nk - 1))
                                        T.inc(ins)
                                block.tensor(fB)
                                av_tok[js] = T.n
                                gidx += 1

                            def fvdB(v, av3=av_tok[3]):
                                v.wait_ge(T.h, av3)
                                ins = None
                                for js in range(4):
                                    ins = v.tensor_copy(
                                        dn4[0:1, js * 512:(js + 1) * 512],
                                        pav[js][64:65, :])
                                V.inc(ins)
                            block.vector(fvdB)
                            dn_tok = V.n

                            def fvd2B(v):
                                v.drain()
                                v.reciprocal(dn4[:], dn4[:])
                                v.drain()
                                ins = v.tensor_copy(rd4[:], dn4[:])
                                V.inc(ins)
                            block.vector(fvd2B)
                            sd_last = V.n
                            rdb_tok = V.n

                            for js in range(4):
                                def ftrd(tn, js=js, rdb_tok=rdb_tok,
                                         srd_prev=srd_last):
                                    tn.wait_ge(V.h, rdb_tok)
                                    if srd_prev is not None:
                                        tn.wait_ge(V.h, srd_prev)
                                    ins = tn.matmul(
                                        prd[:], ones_row[0:1, 0:64],
                                        rd4[0:1, js * 512:(js + 1) * 512],
                                        start=True, stop=True)
                                    T.inc(ins)
                                block.tensor(ftrd)
                                prd_tok = T.n

                                def fvsrd(v, js=js, prd_tok=prd_tok):
                                    v.wait_ge(T.h, prd_tok)
                                    ins = v.tensor_copy(
                                        prd_sb4[:, js * 512:(js + 1) * 512],
                                        prd[:])
                                    V.inc(ins)
                                block.vector(fvsrd)
                                srd_last = V.n
                                prds_tok = V.n

                                def fvao(v, cc=cc, row0=row0, js=js,
                                         prds_tok=prds_tok):
                                    v.wait_ge(V.h, prds_tok)
                                    ins = v.tensor_mul(
                                        attnout[row0:row0 + 64,
                                                cc * TOK + js * 512:
                                                cc * TOK + js * 512 + 512],
                                        pav[js][0:64, :],
                                        prd_sb4[:, js * 512:(js + 1) * 512])
                                    V.inc(ins)
                                block.vector(fvao)
                                ao_ev[h8 * 4 + js] = V.n
                            ao_last = V.n
                        core_s_last = S.n
                ao_tok = V.n
                p7_s_end = S.n
            dtap("attnout", [128, NCH * TOK], BF16)
            dump("attnout", attnout[:], V.h, ao_tok)

            with ExitStack() as p8:
                wo_b = sbuf("wo_b", [128, NCH * 1024], BF16, p8)
                borow = sbuf("p8_bo", [1, D], F32, p8)
                borow_b = sbuf("p8_bob", [1, D], BF16, p8)
                boh = sbuf("p8_boh", [128, D], F32, p8)
                g1l = [sbuf(f"p8_g1{i}", [128, D], BF16, p8) for i in range(2)]
                xo = [sbuf(f"p8_xo{i}", [128, D], F32, p8) for i in range(2)]
                t1s = [sbuf(f"p8_t1{i}", [128, D], F32, p8) for i in range(2)]
                pso = [psum(f"p8_ps{i}", [128, 1024], F32, p8) for i in range(2)]
                pbo = psum("p8_pbo", [128, 512], F32, p8)

                WOL = Tok("sem_wol")
                BOL = Tok("sem_bol")

                def fgw(sy, te=T.n, tv=V.n, ts=S.n):
                    sy.wait_ge(PC.h, pc["wo"])
                    sy.wait_ge(T.h, te)
                    sy.wait_ge(V.h, tv)
                    sy.wait_ge(S.h, ts)
                    sy.dma_start(
                        out=wo_b[:].rearrange("p (c i) -> p c i", c=NCH),
                        in_=woB.rearrange("(c p) i -> p c i", p=128)
                    ).then_inc(WOL.h, 16)
                    WOL.n += 16
                block.sync(fgw)

                def fgb2(g, te=T.n):
                    g.wait_ge(T.h, te)
                    g.dma_start(out=borow[:], in_=bo_in).then_inc(BOL.h, 16)
                    BOL.n += 16
                block.gpsimd(fgb2)

                def fvbo(v):
                    v.wait_ge(BOL.h, 16)
                    ins = v.tensor_copy(borow_b[:], borow[:])
                    V.inc(ins)
                block.vector(fvbo)
                bob_tok = V.n

                boev = {}
                for nn in range(2):
                    def ftbo(tn, nn=nn):
                        if nn == 0:
                            tn.wait_ge(V.h, bob_tok)
                        else:
                            tn.wait_ge(S.h, boev[0])
                        ins = tn.matmul(pbo[:], ones_row[:],
                                        borow_b[:, nn * 512:(nn + 1) * 512],
                                        start=True, stop=True)
                        T.inc(ins)
                    block.tensor(ftbo)
                    tbo = T.n

                    def fsbo(s, nn=nn, tbo=tbo):
                        s.wait_ge(T.h, tbo)
                        ins = s.activation(boh[:, nn * 512:(nn + 1) * 512],
                                           pbo[:], ACT.Copy, scale=0.5)
                        S.inc(ins)
                    block.scalar(fsbo)
                    boev[nn] = S.n
                boh_tok = boev[1]

                fuse_v = {}
                do_tok = {}
                prev_rel = {}
                for t in range(NT):
                    q = t % 2

                    def fsyg(sy, t=t, q=q, pr=prev_rel.get(t - 2),
                             tv=V.n, ts=S.n, te=T.n):
                        if t < 2:
                            sy.wait_ge(V.h, tv)
                            sy.wait_ge(S.h, ts)
                            sy.wait_ge(T.h, te)
                        if t == 0:
                            sy.wait_ge(*g1_done[0])
                            sy.wait_ge(*g1_done[1])
                        if t >= 2:
                            sy.wait_ge(V.h, fuse_v[t - 2])
                            sy.wait_ge(*pr)
                        sy.dma_start(out=g1l[q][:],
                                     in_=g1_spill[t * 128:(t + 1) * 128, :]
                                     ).then_inc(DXS[q].h, 16)
                        DXS[q].n += 16
                    block.sync(fsyg)
                    gl_tok = (DXS[q].h, DXS[q].n)
                    prev_rel[t] = gl_tok

                    def fto(tn, t=t, q=q):
                        if t == 0:
                            tn.wait_ge(WOL.h, 16)
                            tn.wait_ge(V.h, ao_tok)
                            tn.wait_ge(S.h, p7_s_end)
                        if t >= 2:
                            tn.wait_ge(V.h, fuse_v[t - 2])
                        ins = None
                        for c in range(NCH):
                            for nn in range(2):
                                ins = tn.matmul(
                                    pso[q][:, nn * 512:(nn + 1) * 512],
                                    attnout[:, c * TOK + t * 128:
                                            c * TOK + t * 128 + 128],
                                    wo_b[:, c * 1024 + nn * 512:
                                         c * 1024 + nn * 512 + 512],
                                    start=(c == 0), stop=(c == NCH - 1))
                        T.inc(ins)
                    block.tensor(fto)
                    ttok = T.n

                    def fvo2(v, t=t, q=q, ttok=ttok, gl_tok=gl_tok):
                        v.wait_ge(T.h, ttok)
                        v.wait_ge(*gl_tok)
                        if t == 0:
                            v.wait_ge(S.h, boh_tok)
                        if t >= 2:
                            v.wait_ge(DO.h, do_tok[t - 2])
                        v.scalar_tensor_tensor(t1s[q][:], pso[q][:], 0.5,
                                               boh[:], ALU.mult, ALU.add)
                        v.drain()
                        v.tensor_mul(t1s[q][:], t1s[q][:], g1l[q][:])
                        v.drain()
                        ins = v.tensor_add(xo[q][:],
                                           x_new[:, t * D:(t + 1) * D],
                                           t1s[q][:])
                        V.inc(ins)
                    block.vector(fvo2)
                    fuse_v[t] = V.n
                    vtok = V.n

                    def fsyo(sy, t=t, q=q, vtok=vtok, dprev=DO.n):
                        sy.wait_ge(DO.h, dprev)
                        sy.wait_ge(V.h, vtok)
                        sy.dma_start(out=x_out[t * 128:(t + 1) * 128, :],
                                     in_=xo[q][:]).then_inc(DO.h, 16)
                        DO.n += 16
                    block.sync(fsyo)
                    do_tok[t] = DO.n

        def ffin(sy):
            sy.wait_ge(DO.h, DO.n)
        block.sync(ffin)

    return nc, dbg_out



def _stage_inputs(inputs):
    f = np.ascontiguousarray
    i32 = np.float32
    common = {}
    common["ident_in"] = np.eye(128, dtype=i32)
    cm = np.zeros((128, 4 * 512), dtype=i32)
    kk = np.arange(128)[:, None]
    qq = np.arange(512)[None, :]
    for ri, r in enumerate((0, 128, 256, 384)):
        cm[:, ri * 512:(ri + 1) * 512] = ((kk + r) <= qq).astype(i32)
    common["causal_in"] = cm
    common["ones_in"] = np.ones((1, 128), dtype=i32)
    pat = np.zeros((16, NCH * 128), dtype=i32)
    for c in range(NCH):
        for p in range(128):
            pat[2 * c + p // 64, c * 128 + p] = 1.0
    common["pat16_in"] = pat
    ld = np.asarray(inputs["L0_attn_log_decay"], dtype=i32)
    ldc = np.zeros((128, NCH), dtype=i32)
    for c in range(NCH):
        for p in range(128):
            ldc[p, c] = ld[2 * c + p // 64]
    common["ld_cols"] = ldc
    q_ = np.asarray(inputs["L0_attn_q"], dtype=i32).reshape(16, 64).T
    k_ = np.asarray(inputs["L0_attn_k"], dtype=i32).reshape(16, 64).T
    common["qk_in"] = f(np.concatenate([q_, k_], axis=1))
    common["nw_cols"] = f(np.asarray(
        inputs["L0_attn_norm_weight"], dtype=i32).reshape(NCH, 128).T)
    common["owT"] = f(np.asarray(inputs["L0_attn_o_w"], dtype=i32).T)
    common["wgsT"] = f(np.asarray(inputs["L0_ffn_Wg_s"], dtype=i32).T)
    common["wusT"] = f(np.asarray(inputs["L0_ffn_Wu_s"], dtype=i32).T)
    common["wdsT"] = f(np.asarray(inputs["L0_ffn_Wd_s"], dtype=i32).T)
    common["wgT"] = f(np.asarray(inputs["L0_ffn_Wg"], dtype=i32).transpose(0, 2, 1))
    common["wuT"] = f(np.asarray(inputs["L0_ffn_Wu"], dtype=i32).transpose(0, 2, 1))
    common["wdT"] = f(np.asarray(inputs["L0_ffn_Wd"], dtype=i32).transpose(0, 2, 1))
    common["wrT"] = f(np.asarray(inputs["L0_ffn_Wr"], dtype=i32).T)
    common["rb_in"] = f(np.asarray(inputs["L0_ffn_rb"], dtype=i32).reshape(1, NE0))
    common["wqT"] = f(np.asarray(inputs["L1_attn_Wq"], dtype=i32).T)
    common["wkT"] = f(np.asarray(inputs["L1_attn_Wk"], dtype=i32).T)
    common["wvT"] = f(np.asarray(inputs["L1_attn_Wv"], dtype=i32).T)
    common["woT"] = f(np.asarray(inputs["L1_attn_Wo"], dtype=i32).T)
    common["bq_in"] = f(np.asarray(inputs["L1_attn_bq"], dtype=i32).reshape(1, D))
    common["bk_in"] = f(np.asarray(inputs["L1_attn_bk"], dtype=i32).reshape(1, D))
    common["bv_in"] = f(np.asarray(inputs["L1_attn_bv"], dtype=i32).reshape(1, D))
    common["bo_in"] = f(np.asarray(inputs["L1_attn_bo"], dtype=i32).reshape(1, D))
    common["se_in"] = f(np.asarray(
        inputs["L1_ffn_slot_embeds"], dtype=i32).reshape(E1 * S1, D))
    common["ng_in"] = f(np.asarray(inputs["L1_ffn_norm_g"], dtype=i32).reshape(1, D))
    common["nb_in"] = f(np.asarray(inputs["L1_ffn_norm_b"], dtype=i32).reshape(1, D))
    common["sg_in"] = f(np.asarray(inputs["L1_ffn_slot_g"], dtype=i32).reshape(1, D))
    common["sb_in"] = f(np.asarray(inputs["L1_ffn_slot_b"], dtype=i32).reshape(1, D))
    common["sc1_in"] = f(np.asarray(inputs["L1_ffn_scale"], dtype=i32).reshape(1, 1))
    common["wg1"] = f(np.asarray(inputs["L1_ffn_Wg"], dtype=i32))
    common["wu1"] = f(np.asarray(inputs["L1_ffn_Wu"], dtype=i32))
    common["wd1"] = f(np.asarray(inputs["L1_ffn_Wd"], dtype=i32))
    common["bg1"] = f(np.asarray(inputs["L1_ffn_bg"], dtype=i32).reshape(1, E1 * F1))
    common["bu1"] = f(np.asarray(inputs["L1_ffn_bu"], dtype=i32).reshape(1, E1 * F1))
    common["bd1"] = f(np.asarray(inputs["L1_ffn_bd"], dtype=i32).reshape(1, E1 * D))

    x = np.asarray(inputs["L0_attn_x"], dtype=i32)
    in_maps = []
    for b in range(8):
        m = dict(common)
        m["x_in"] = f(x[b])
        in_maps.append(m)
    return in_maps


_PROG_CACHE = {}


def _get_prog(debug=()):
    key = tuple(sorted(debug))
    if key not in _PROG_CACHE:
        _PROG_CACHE[key] = build_core_program(debug=key)
    return _PROG_CACHE[key]


def kernel(**inputs):
    nc, _ = _get_prog()
    in_maps = _stage_inputs(inputs)
    res = run_bass_kernel_spmd(nc, in_maps, core_ids=list(range(8)))
    out = np.stack([np.asarray(r["x_out"]) for r in res.results], axis=0)
    return out.astype(np.float32)
